# version 03 (final)
# baseline (speedup 1.0000x reference)
import sys
sys.path.insert(0, "/opt/trn_rl_repo")
from contextlib import ExitStack

import numpy as np
import ml_dtypes

import concourse.bass as bass
import concourse.mybir as mybir
from concourse.bass_utils import run_bass_kernel_spmd

F32 = mybir.dt.float32
BF16 = mybir.dt.bfloat16
AF = mybir.ActivationFunctionType
ALU = mybir.AluOpType
AX = mybir.AxisListType

BF = ml_dtypes.bfloat16

B, T, D = 4, 4096, 1024
NCORES = 8
OWN = 2048
NT = OWN // 128

ENGINES = ("t", "v", "s", "g", "q")
import os
FORCE_SYNC = bool(int(os.environ.get("MK_FORCE_SYNC", "0")))


def own_rows(parity):
    return [(0, 1024), (3072, 4096)] if parity == 0 else [(1024, 3072)]


def own_index(parity):
    idx = []
    for a, b_ in own_rows(parity):
        idx.extend(range(a, b_))
    return np.array(idx)


def tile_pos(parity, t):
    if parity == 0:
        return t * 128 if t < 8 else 3072 + (t - 8) * 128
    return 1024 + t * 128


def ceil_div(a, b):
    return -(-a // b)



class Op:
    __slots__ = ("eng", "fn", "inc_n", "is_async", "deps", "seq", "needs_inc",
                 "cum", "tag", "sclass")

    def __init__(self, eng, fn, inc_n, is_async, tag):
        self.eng = eng
        self.fn = fn
        self.inc_n = inc_n
        self.is_async = is_async
        self.deps = []
        self.seq = 0
        self.needs_inc = False
        self.cum = 0
        self.tag = tag


class Prog:

    def __init__(self, name="prog"):
        self.nc = bass.Bass("TRN2", target_bir_lowering=False)
        self.stack = ExitStack()
        self.ops = {e: [] for e in ENGINES}
        self.all_ops = []
        self.last_writer = {}
        self.readers = {}
        self.tensors = {}
        self._semh = {}

    def sb(self, name, shape, dtype):
        h = self.stack.enter_context(self.nc.sbuf_tensor(name, shape, dtype))
        self.tensors[name] = h
        return h

    def psum(self, name, shape, dtype=F32):
        h = self.stack.enter_context(self.nc.psum_tensor(name, shape, dtype))
        self.tensors[name] = h
        return h

    def din(self, name, shape, dtype=F32):
        return self.nc.dram_tensor(name, list(shape), dtype, kind="ExternalInput").ap()

    def dout(self, name, shape, dtype=F32):
        return self.nc.dram_tensor(name, list(shape), dtype, kind="ExternalOutput").ap()

    def dtmp(self, name, shape, dtype=F32):
        return self.nc.dram_tensor(name, list(shape), dtype, kind="Internal").ap()

    def op(self, eng, fn, reads=(), writes=(), dma=False, tag=None):
        o = Op(eng, fn, 16 if dma else 1, dma, tag)
        if eng == "s" and tag != "memset_const":
            reads = list(reads) + [("fconsts",)]
        deps = []
        for r in reads:
            w = self.last_writer.get(r)
            if w is not None:
                deps.append(w)
        for w_ in writes:
            lw = self.last_writer.get(w_)
            if lw is not None:
                deps.append(lw)
            deps.extend(self.readers.get(w_, ()))
        seen = set()
        for d in deps:
            if d is not o and id(d) not in seen:
                seen.add(id(d))
                o.deps.append(d)
        for r in reads:
            self.readers.setdefault(r, []).append(o)
        for w_ in writes:
            self.last_writer[w_] = o
            self.readers[w_] = []
        o.seq = len(self.ops[eng])
        self.ops[eng].append(o)
        self.all_ops.append(o)
        if os.environ.get("MK_TRACE_OPS") and (
                any(isinstance(k, tuple) and k and k[0] in ("psum", "ptr")
                    for k in list(reads) + list(writes))):
            print(f"OPTRACE {eng} {tag} R={[k for k in reads if isinstance(k, tuple) and k[0] in ('psum','ptr')]} "
                  f"W={[k for k in writes if isinstance(k, tuple) and k[0] in ('psum','ptr')]}")
        return o

    def dma(self, eng, dst_ap, src_ap, reads=(), writes=(), tag=None, **kw):
        return self.op(
            eng, lambda e: e.dma_start(out=dst_ap, in_=src_ap, **kw),
            reads=reads, writes=writes, dma=True, tag=tag)

    def push_scope(self):
        sc = ExitStack()
        if not hasattr(self, "_scopes"):
            self._scopes = []
        self._scopes.append(sc)
        return sc

    def pop_scope(self):
        self.barrier()
        sc = self._scopes.pop()
        sc.close()

    def sbs(self, name, shape, dtype):
        h = self._scopes[-1].enter_context(self.nc.sbuf_tensor(name, shape, dtype))
        return h

    def barrier(self):
        tails = {e: (self.ops[e][-1] if self.ops[e] else None) for e in ENGINES}
        for e in ENGINES:
            o = self.op(e, lambda eng: eng.nop(), tag="barrier")
            for e2, t in tails.items():
                if t is not None and t is not o:
                    o.deps.append(t)

    def collective(self, kind, op_, replica_groups, ins, outs, reads=(),
                   writes=(), tag=None):
        return self.op(
            "g", lambda e: e.collective_compute(
                kind, op_, replica_groups, ins=ins, outs=outs),
            reads=reads, writes=writes, dma=True, tag=tag or f"cc_{kind}")

    def build(self):
        def needs_edge(prod, cons):
            if prod.eng != cons.eng or prod.is_async:
                return True
            if FORCE_SYNC:
                return True
            return prod.eng != "t"

        for o in self.all_ops:
            if o.is_async:
                o.needs_inc = True
            for d in o.deps:
                if needs_edge(d, o):
                    d.needs_inc = True
        self._needs_edge = needs_edge
        NLANES = 8
        lane_idx = {}
        for e in ENGINES:
            for o in self.ops[e]:
                if o.is_async:
                    i = lane_idx.get(e, 0)
                    lane_idx[e] = i + 1
                    o.sclass = (e, "d", i % NLANES)
                else:
                    o.sclass = (e,)
        cnt = {}
        for e in ENGINES:
            for o in self.ops[e]:
                k = o.sclass
                if o.needs_inc:
                    cnt[k] = cnt.get(k, 0) + o.inc_n
                o.cum = cnt.get(k, 0)
        for k in cnt:
            nm = f"sem_{k[0]}" + (f"d{k[2]}" if len(k) > 1 else "")
            self._semh[k] = self.stack.enter_context(self.nc.semaphore(nm))

        secs = {"t": "tensor", "v": "vector", "s": "scalar",
                "g": "gpsimd", "q": "sync"}
        with self.nc.Block() as block:
            for e in ENGINES:
                if not self.ops[e]:
                    continue

                def make(eng_key):
                    def emit(engh):
                        waited = {}
                        for o in self.ops[eng_key]:
                            for d in o.deps:
                                if not self._needs_edge(d, o):
                                    continue
                                k = d.sclass
                                tgt = d.cum
                                assert d.needs_inc and tgt > 0
                                if waited.get(k, 0) < tgt:
                                    engh.wait_ge(self._semh[k], tgt)
                                    waited[k] = tgt
                            if o.is_async and o.cum > o.inc_n:
                                k = o.sclass
                                if waited.get(k, 0) < o.cum - o.inc_n:
                                    engh.wait_ge(self._semh[k], o.cum - o.inc_n)
                                    waited[k] = o.cum - o.inc_n
                            instr = o.fn(engh)
                            if o.needs_inc:
                                if instr is None:
                                    raise RuntimeError(
                                        f"op {o.tag} on {eng_key} needs inc "
                                        f"but emitted no instruction")
                                instr.then_inc(self._semh[o.sclass], o.inc_n)
                        for k, h in self._semh.items():
                            if len(k) > 1 and k[0] == eng_key:
                                fin = cnt.get(k, 0)
                                if fin > 0 and waited.get(k, 0) < fin:
                                    engh.wait_ge(h, fin)
                    return emit

                getattr(block, secs[e])(make(e))
        self.stack.close()
        return self.nc



class Core:

    def __init__(self, p: Prog):
        self.p = p
        self.nc = p.nc
        self.identb = p.sb("identb", [128, 128], BF16)
        self.identf = p.sb("identf", [128, 128], F32)
        self.ones_row = p.sb("ones_row", [1, 512], BF16)
        self.ones_row_f = p.sb("ones_row_f", [1, 512], F32)
        self.ones_col_f = p.sb("ones_col_f", [128, 1], F32)
        self.ones_col = p.sb("ones_col", [128, 1], BF16)
        self._const_dram = None
        self._pb = [p.psum(f"pb{i}", [128, 512], F32) for i in range(6)]
        self._pb_next = 0
        self._ptr = [p.psum(f"ptr{i}", [128, 1024], BF16) for i in range(2)]
        self._ptr_next = 0
        self._scr = {}

    def load_consts(self, const_in):
        p = self.p
        p.dma("q", self.identf[:], const_in[0:128, 0:128], writes=[("identf",)])
        p.dma("g", self.identb[:], const_in[0:128, 0:128], writes=[("identb",)])
        p.dma("g", self.ones_row[:], const_in[128:129, 0:512], writes=[("ones_row",)])
        p.dma("q", self.ones_row_f[:], const_in[128:129, 0:512], writes=[("ones_row_f",)])
        p.op("g", lambda e: e.memset(self.ones_col_f[:], 1.0),
             writes=[("ones_col_f",)], tag="ones_col_f")
        p.op("g", lambda e: e.memset(self.ones_col[:], 1.0),
             writes=[("ones_col",)], tag="ones_col")

    def bank(self):
        i = self._pb_next
        self._pb_next = (self._pb_next + 1) % len(self._pb)
        return self._pb[i], ("psum", i)

    def tbank(self):
        i = self._ptr_next
        self._ptr_next = (self._ptr_next + 1) % len(self._ptr)
        return self._ptr[i], ("ptr", i)

    def pbank(self, i):
        return self._pb[i], ("psum", i)

    def scratch(self, name, shape, dtype):
        key = (name, tuple(shape), dtype)
        if key not in self._scr:
            self._scr[key] = self.p.sb(name, shape, dtype)
        return self._scr[key]


def np_consts():
    c = np.zeros((641, 1536), dtype=np.float32)
    c[0:128, 0:128] = np.eye(128, dtype=np.float32)
    c[128, :] = 1.0
    i = np.arange(128)[:, None]
    cc = np.arange(1536)[None, :]
    c[129:257, :] = np.where(cc <= i + 384, 0.0, -1e30)
    c[257:385, :] = np.where(i <= cc - 384, 0.0, -1e30)
    c[385:513, :] = np.where((cc >= i + 384) & (cc <= i + 640), 0.0, -1e30)
    c[513:641, :] = np.where((i >= cc - 896) & (i <= cc - 640), 0.0, -1e30)
    return c



def emit_ln_tile(cr, x_ap, xkeys, out_specs, Dw=1024, eps=1e-5, sidx=0):
    p = cr.p
    stats = cr.scratch(f"lnstats{sidx}", [128, 8], F32)
    sq = cr.scratch("lnsq", [128, 1024], F32)
    assert Dw <= 1024
    sq = sq[:, 0:Dw] if Dw < 1024 else sq
    skey = ("lnstats", sidx)
    sqkey = ("lnsq",)
    xkeys = list(xkeys)
    p.op("v", lambda e: e.reduce_sum(stats[:, 0:1], x_ap, axis=AX.X),
         reads=xkeys, writes=[skey], tag="ln_sum")
    p.op("v", lambda e: e.tensor_scalar_mul(stats[:, 1:2], stats[:, 0:1], 1.0 / Dw),
         reads=[skey], writes=[skey])
    p.op("s", lambda e: e.activation(sq[:], x_ap, AF.Square,
                                     accum_out=stats[:, 3:4]),
         reads=xkeys, writes=[sqkey, skey], tag="ln_sq")
    p.op("v", lambda e: e.tensor_scalar_mul(stats[:, 4:5], stats[:, 3:4], 1.0 / Dw),
         reads=[skey], writes=[skey])
    p.op("v", lambda e: e.tensor_mul(stats[:, 5:6], stats[:, 1:2], stats[:, 1:2]),
         reads=[skey], writes=[skey])
    p.op("v", lambda e: e.tensor_sub(stats[:, 4:5], stats[:, 4:5], stats[:, 5:6]),
         reads=[skey], writes=[skey])
    p.op("s", lambda e: e.activation(stats[:, 6:7], stats[:, 4:5], AF.Sqrt,
                                     bias=float(eps)),
         reads=[skey, fconst_key(cr, eps)], writes=[skey], tag="ln_sqrt")
    p.op("v", lambda e: e.reciprocal(stats[:, 7:8], stats[:, 6:7]),
         reads=[skey], writes=[skey])
    for out_ap, okey in out_specs:
        p.op("v", lambda e, out_ap=out_ap: e.tensor_scalar(
            out_ap, x_ap, stats[:, 1:2], stats[:, 7:8],
            ALU.subtract, ALU.mult),
            reads=xkeys + [skey], writes=[okey], tag="ln_apply")



def register_float_consts(cr, values):
    p = cr.p
    n = len(values)
    tile = p.sb("floatconsts", [128, n], F32)
    for i, v in enumerate(values):
        p.op("g", lambda e, i=i, v=v: e.memset(tile[:, i:i + 1], float(v)),
             writes=[("fconst", i), ("fconsts",)], tag="memset_const")
        p.nc.const_aps.register(F32, float(v), tile[:, i:i + 1])
    cr.fconst_tile = tile
    cr.fconst_map = {float(v): i for i, v in enumerate(values)}
    return tile


def fconst_key(cr, v):
    return ("fconst", cr.fconst_map[float(v)])



def emit_tok_to_fm(cr, src_tile_ap, srckeys, t, dst_fm, dstkey, ntok=OWN,
                   evac_eng="v", nslab_max=8):
    p = cr.p
    Dw = src_tile_ap.shape[-1]
    nslab = Dw // 128
    srckeys = list(srckeys)
    for s0 in range(0, nslab, nslab_max):
        ns = min(nslab_max, nslab - s0)
        bank, bkey = cr.tbank()
        for s_ in range(ns):
            src = src_tile_ap[:, (s0 + s_) * 128:(s0 + s_ + 1) * 128]
            p.op("t", lambda e, src=src, s_=s_, bank=bank: e.transpose(
                bank[:, s_ * 128:(s_ + 1) * 128], src, cr.identb[:]),
                reads=srckeys + [("identb",)], writes=[bkey], tag="t2fm_tr")
        dst = dst_fm.ap().rearrange("p (s n) -> p s n", n=ntok)[
            :, s0:s0 + ns, t * 128: t * 128 + 128]
        srcv = bank.ap().rearrange("p (s n) -> p s n", s=nslab_max)[:, 0:ns, :]
        p.op(evac_eng, lambda e, dst=dst, srcv=srcv: e.tensor_copy(dst, srcv),
             reads=[bkey], writes=[dstkey], tag="t2fm_cp")


def emit_tr_f32(cr, src_ap, dst_sb_ap, reads, writes):
    p = cr.p
    bank, bkey = cr.bank()
    p.op("t", lambda e: e.matmul(bank[:, 0:128], src_ap, cr.identf[:],
                                 start=True, stop=True, is_transpose=True),
         reads=list(reads) + [("identf",)], writes=[bkey], tag="trf_tr")
    p.op("v", lambda e: e.tensor_copy(dst_sb_ap, bank[:, 0:128]),
         reads=[bkey], writes=list(writes), tag="trf_cp")



FLOAT_CONSTS = [1e-5, 1e-6, 1e-12, 1e-8, 0.0, 1.0, -1.0, -0.5]


def begin_launch(extra_consts=()):
    p = Prog()
    x_in = p.din("x_in", (OWN, D))
    consts = p.din("consts", (641, 1536))
    x_out = p.dout("x_out", (OWN, D))
    cr = Core(p)
    register_float_consts(cr, list(FLOAT_CONSTS) + list(extra_consts))
    cr.load_consts(consts)
    x_sb = p.sb("x_sb", [128, NT * D], F32)
    cr.x_sb = x_sb
    p.dma("q", x_sb.ap().rearrange("p (t d) -> p t d", t=NT),
          x_in.rearrange("(t p) d -> p t d", p=128),
          writes=[("x", t) for t in range(NT)], tag="load_x")
    return p, cr, {"x_in": x_in, "x_out": x_out, "consts": consts}


def finish_launch(p, cr, aps):
    p.dma("q", aps["x_out"].rearrange("(t p) d -> p t d", p=128),
          cr.x_sb.ap().rearrange("p (t d) -> p t d", t=NT),
          reads=[("x", t) for t in range(NT)], tag="store_x")
    st_op = p.all_ops[-1]
    st_op.needs_inc = True
    p.op("q", lambda e: e.nop(), reads=[], writes=[], tag="final_nop")
    p.all_ops[-1].deps.append(st_op)
    return p.build()


def emit_residual_ln_add(cr, y_ap, ykeys, t, sidx=0):
    p = cr.p
    yn = cr.scratch(f"ynscr{sidx}", [128, D], F32)
    ynkey = ("ynscr", sidx)
    emit_ln_tile(cr, y_ap, ykeys, [(yn[:], ynkey)], Dw=D, sidx=sidx)
    xt = cr.x_sb[:, t * D:(t + 1) * D]
    p.op("v", lambda e: e.tensor_add(xt, xt, yn[:]),
         reads=[ynkey, ("x", t)], writes=[("x", t)], tag="resid_add")



def emit_moe_gating(cr, nexp, gateT_sb, gb_sb, t, xnf_tile_ap, xnf_key,
                    ge_sb, mode, eps_norm=1e-8, f32mm=True):
    p = cr.p
    kind, K = mode
    xnff = cr.scratch("gat_xnff", [128, D], F32)
    for s_ in range(D // 128):
        emit_tr_f32(cr, xnf_tile_ap[:, s_ * 128:(s_ + 1) * 128],
                    xnff[:, s_ * 128:(s_ + 1) * 128],
                    reads=[xnf_key], writes=[("gat_xnff", s_)])
    bank, bkey = cr.bank()
    nsl = D // 128
    for s_ in range(nsl):
        p.op("t", lambda e, s_=s_: e.matmul(
            bank[:, 0:nexp], xnff[:, s_ * 128:(s_ + 1) * 128],
            gateT_sb[:, s_ * nexp:(s_ + 1) * nexp],
            start=(s_ == 0), stop=False),
            reads=[("gat_xnff", s_), ("gateT",)], writes=[bkey], tag="gat_mm")
    p.op("t", lambda e: e.matmul(
        bank[:, 0:nexp], cr.ones_row_f[0:1, 0:128], gb_sb[0:1, 0:nexp],
        start=False, stop=True),
        reads=[("gb",), ("ones_row_f",)], writes=[bkey], tag="gat_bias")
    sg = cr.scratch("gat_s", [128, 16], F32)
    wrk = cr.scratch("gat_w", [128, 64], F32)
    gkey = ("gat_s",)
    wkey = ("gat_wrk",)
    if kind == "sigmoid_topk":
        p.op("s", lambda e: e.activation(sg[:, 0:nexp], bank[:, 0:nexp],
                                         AF.Sigmoid),
             reads=[bkey], writes=[gkey], tag="gat_sig")
    else:
        p.op("v", lambda e: e.reduce_max(wrk[:, 0:1], bank[:, 0:nexp],
                                         axis=AX.X, negate=True),
             reads=[bkey], writes=[wkey])
        p.op("s", lambda e: e.activation(sg[:, 0:nexp], bank[:, 0:nexp],
                                         AF.Exp, bias=wrk[:, 0:1],
                                         accum_out=wrk[:, 1:2]),
             reads=[bkey, wkey], writes=[gkey, wkey], tag="gat_exp")
        p.op("v", lambda e: e.reciprocal(wrk[:, 2:3], wrk[:, 1:2]),
             reads=[wkey], writes=[wkey])
        p.op("v", lambda e: e.tensor_scalar_mul(sg[:, 0:nexp], sg[:, 0:nexp],
                                                wrk[:, 2:3]),
             reads=[gkey, wkey], writes=[gkey])
    emit_topk_ge(cr, nexp, K, t, ge_sb, eps_norm=eps_norm)


def emit_topk_ge(cr, nexp, K, t, ge_sb, eps_norm=0.0):
    p = cr.p
    sg = cr.scratch("gat_s", [128, 16], F32)
    wrk = cr.scratch("gat_w", [128, 64], F32)
    gkey = ("gat_s",)
    wkey = ("gat_wrk",)
    sw = wrk[:, 8:8 + nexp]
    macc = wrk[:, 24:24 + nexp]
    mcur = wrk[:, 40:40 + nexp]
    msum = wrk[:, 56:57]
    p.op("v", lambda e: e.tensor_copy(sw, sg[:, 0:nexp]),
         reads=[gkey], writes=[wkey])
    p.op("v", lambda e: e.memset(macc, 0.0), writes=[wkey])
    p.op("v", lambda e: e.memset(msum[:], 0.0), writes=[wkey])
    for k in range(K):
        p.op("v", lambda e: e.reduce_max(wrk[:, 4:5], sw, axis=AX.X),
             reads=[wkey], writes=[wkey])
        p.op("v", lambda e: e.tensor_single_scalar(mcur, sw, wrk[:, 4:5],
                                                   ALU.is_ge),
             reads=[wkey], writes=[wkey])
        p.op("v", lambda e: e.tensor_add(macc, macc, mcur),
             reads=[wkey], writes=[wkey])
        p.op("v", lambda e: e.tensor_add(msum[:], msum[:], wrk[:, 4:5]),
             reads=[wkey], writes=[wkey])
        if k < K - 1:
            p.op("v", lambda e: e.tensor_scalar(mcur, mcur, -1.0, 1.0,
                                                ALU.mult, ALU.add),
                 reads=[wkey], writes=[wkey])
            p.op("v", lambda e: e.tensor_mul(sw, sw, mcur),
                 reads=[wkey], writes=[wkey])
    p.op("v", lambda e: e.tensor_scalar_add(msum[:], msum[:], float(eps_norm)),
         reads=[wkey], writes=[wkey])
    p.op("v", lambda e: e.reciprocal(wrk[:, 5:6], msum[:]),
         reads=[wkey], writes=[wkey])
    ge = ge_sb[:, t * nexp:(t + 1) * nexp]
    p.op("v", lambda e: e.tensor_mul(ge, macc, sg[:, 0:nexp]),
         reads=[wkey, gkey], writes=[("ge", t)], tag="gat_ge1")
    p.op("v", lambda e: e.tensor_scalar_mul(ge, ge, wrk[:, 5:6]),
         reads=[("ge", t), wkey], writes=[("ge", t)], tag="gat_ge2")


def emit_glu_expert(cr, xn_fm, w1buf, w1key, w3buf, w3key, h_fm, hkey,
                    tg, hs, act=AF.Silu):
    p = cr.p
    g_scr = cr.scratch("glu_g", [128, 512], BF16)
    u_scr = cr.scratch("glu_u", [128, 512], BF16)
    for sub in range(2):
        tok0 = tg * 1024 + sub * 512
        bg, bgk = cr.bank()
        for s_ in range(8):
            p.op("t", lambda e, s_=s_, bg=bg, tok0=tok0: e.matmul(
                bg[:], w1buf[:, s_ * 128:(s_ + 1) * 128],
                xn_fm.ap().rearrange("p (s n) -> p s n", n=OWN)[:, s_, tok0:tok0 + 512],
                start=(s_ == 0), stop=(s_ == 7)),
                reads=[w1key, ("xnfm",)], writes=[bgk], tag="glu_mm1")
        bu, buk = cr.bank()
        for s_ in range(8):
            p.op("t", lambda e, s_=s_, bu=bu, tok0=tok0: e.matmul(
                bu[:], w3buf[:, s_ * 128:(s_ + 1) * 128],
                xn_fm.ap().rearrange("p (s n) -> p s n", n=OWN)[:, s_, tok0:tok0 + 512],
                start=(s_ == 0), stop=(s_ == 7)),
                reads=[w3key, ("xnfm",)], writes=[buk], tag="glu_mm3")
        p.op("s", lambda e, bg=bg: e.activation(g_scr[:], bg[:], act),
             reads=[bgk], writes=[("glu_g",)], tag="glu_act")
        p.op("s", lambda e, bu=bu: e.copy(u_scr[:], bu[:]),
             reads=[buk], writes=[("glu_u",)], tag="glu_cp")
        hdst = h_fm[:, hs * 1024 + sub * 512: hs * 1024 + sub * 512 + 512]
        p.op("v", lambda e, hdst=hdst: e.tensor_mul(hdst, g_scr[:], u_scr[:]),
             reads=[("glu_g",), ("glu_u",)], writes=[hkey], tag="glu_mul")


def emit_l3_ffn(cr, aps):
    p = cr.p
    w1T = aps["l3f_w1T"]
    w3T = aps["l3f_w3T"]
    w2T = aps["l3f_w2T"]
    fc1T = aps["l3f_fc1T"]
    fc2T = aps["l3f_fc2T"]
    fc3T = aps["l3f_fc3T"]
    gateT = aps["l3f_gateT"]
    gateb = aps["l3f_gateb"]
    NE = 6

    xn_fm = p.sb("xn_fm", [128, 8 * OWN], BF16)
    ge_sb = p.sb("ge_sb", [128, NT * NE], F32)
    y_acc = p.sb("y_acc", [128, 8 * D], F32)
    h_fm = p.sb("h_fm", [128, 8 * 1024], BF16)
    w1buf = p.sb("w1buf", [128, 2 * 8 * 128], BF16)
    w3buf = p.sb("w3buf", [128, 2 * 8 * 128], BF16)
    w2buf = p.sb("w2buf", [128, 8 * 1024], BF16)
    gateT_sb = p.sb("gateT_sb", [128, 8 * NE], F32)
    gb_sb = p.sb("gb_sb", [1, NE], F32)
    xnt = cr.scratch("xnt", [128, D], BF16)
    xnf = cr.scratch("xnf", [128, D], F32)

    p.dma("q", gateT_sb.ap().rearrange("p (s e) -> p s e", e=NE),
          gateT.rearrange("(s p) e -> p s e", p=128), writes=[("gateT",)])
    p.dma("q", gb_sb[:], gateb, writes=[("gb",)])

    for t in range(NT):
        x_ap = cr.x_sb[:, t * D:(t + 1) * D]
        emit_ln_tile(cr, x_ap, [("x", t)],
                     [(xnt[:], ("xnt",)), (xnf[:], ("xnf",))], sidx=t % 2)
        emit_tok_to_fm(cr, xnt[:], [("xnt",)], t, xn_fm, ("xnfm",))
        emit_moe_gating(cr, NE, gateT_sb, gb_sb, t, xnf[:], ("xnf",),
                        ge_sb, ("sigmoid_topk", 2), eps_norm=1e-8)

    def stream_w_chunk(dst_buf, dstkey, srcT, e, hs, buf_i):
        dst = dst_buf.ap().rearrange("p (b s n) -> p b s n", b=2, s=8)[:, buf_i]
        src = srcT[e].rearrange("(s p) h -> p s h", p=128)[:, :, hs * 128:(hs + 1) * 128]
        p.dma("g", dst, src, writes=[dstkey], tag="wchunk")

    for tg in range(2):
        for e in range(NE + 1):
            shared = (e == NE)
            w2src = fc3T if shared else w2T[e]
            p.dma("g", w2buf.ap().rearrange("p (s n) -> p s n", s=8),
                  (w2src if shared else w2src).rearrange("(s p) d -> p s d", p=128),
                  writes=[("w2buf",)], tag="w2load")
            for hs in range(8):
                bi = hs % 2
                w1k = ("w1buf", bi)
                w3k = ("w3buf", bi)
                if shared:
                    dst1 = w1buf.ap().rearrange("p (b s n) -> p b s n", b=2, s=8)[:, bi]
                    p.dma("g", dst1, fc1T.rearrange("(s p) h -> p s h", p=128)
                          [:, :, hs * 128:(hs + 1) * 128], writes=[w1k])
                    dst3 = w3buf.ap().rearrange("p (b s n) -> p b s n", b=2, s=8)[:, bi]
                    p.dma("g", dst3, fc2T.rearrange("(s p) h -> p s h", p=128)
                          [:, :, hs * 128:(hs + 1) * 128], writes=[w3k])
                else:
                    stream_w_chunk(w1buf, w1k, w1T, e, hs, bi)
                    stream_w_chunk(w3buf, w3k, w3T, e, hs, bi)
                w1v = w1buf.ap().rearrange("p (b n) -> p b n", b=2)[:, bi]
                w3v = w3buf.ap().rearrange("p (b n) -> p b n", b=2)[:, bi]
                emit_glu_expert(cr, xn_fm, w1v, w1k, w3v, w3k,
                                h_fm, ("hfm",), tg, hs)
            for tl in range(8):
                t = tg * 8 + tl
                for dg in range(2):
                    by, byk = cr.bank()
                    for hs in range(8):
                        p.op("t", lambda e2, hs=hs, tl=tl, dg=dg, by=by: e2.matmul(
                            by[:], h_fm[:, hs * 1024 + tl * 128: hs * 1024 + tl * 128 + 128],
                            w2buf[:, hs * 1024 + dg * 512: hs * 1024 + dg * 512 + 512],
                            start=(hs == 0), stop=(hs == 7)),
                            reads=[("hfm",), ("w2buf",)], writes=[byk], tag="down_mm")
                    ydst = y_acc[:, tl * D + dg * 512: tl * D + dg * 512 + 512]
                    if shared:
                        p.op("v", lambda e2, ydst=ydst, by=by: e2.tensor_add(
                            ydst, ydst, by[:]),
                            reads=[byk, ("yacc", t, dg)], writes=[("yacc", t, dg)],
                            tag="down_add")
                    elif e == 0:
                        p.op("v", lambda e2, ydst=ydst, by=by, t=t, e=e: e2.tensor_scalar_mul(
                            ydst, by[:], ge_sb[:, t * NE + e: t * NE + e + 1]),
                            reads=[byk, ("ge", t)], writes=[("yacc", t, dg)],
                            tag="down_scale0")
                    else:
                        p.op("v", lambda e2, ydst=ydst, by=by, t=t, e=e: e2.scalar_tensor_tensor(
                            ydst, by[:], ge_sb[:, t * NE + e: t * NE + e + 1],
                            ydst, ALU.mult, ALU.add),
                            reads=[byk, ("ge", t), ("yacc", t, dg)],
                            writes=[("yacc", t, dg)], tag="down_scaleacc")
        for tl in range(8):
            t = tg * 8 + tl
            emit_residual_ln_add(
                cr, y_acc[:, tl * D:(tl + 1) * D],
                [("yacc", t, 0), ("yacc", t, 1)], t, sidx=2 + (tl % 2))


def l3_ffn_host_inputs(inp):
    f32 = np.float32
    return {
        "l3f_w1T": np.ascontiguousarray(np.transpose(inp["L3_ffn_ew1"], (0, 2, 1))).astype(f32),
        "l3f_w3T": np.ascontiguousarray(np.transpose(inp["L3_ffn_ew3"], (0, 2, 1))).astype(f32),
        "l3f_w2T": np.ascontiguousarray(np.transpose(inp["L3_ffn_ew2"], (0, 2, 1))).astype(f32),
        "l3f_fc1T": np.ascontiguousarray(inp["L3_ffn_fc1"].T).astype(f32),
        "l3f_fc2T": np.ascontiguousarray(inp["L3_ffn_fc2"].T).astype(f32),
        "l3f_fc3T": np.ascontiguousarray(inp["L3_ffn_fc3"].T).astype(f32),
        "l3f_gateT": np.ascontiguousarray(inp["L3_ffn_gate_w"].T).astype(f32),
        "l3f_gateb": inp["L3_ffn_gate_b"].reshape(1, 6).astype(f32),
    }


def build_l3_ffn():
    p, cr, aps = begin_launch()
    aps["l3f_w1T"] = p.din("l3f_w1T", (6, 1024, 1024))
    aps["l3f_w3T"] = p.din("l3f_w3T", (6, 1024, 1024))
    aps["l3f_w2T"] = p.din("l3f_w2T", (6, 1024, 1024))
    aps["l3f_fc1T"] = p.din("l3f_fc1T", (1024, 1024))
    aps["l3f_fc2T"] = p.din("l3f_fc2T", (1024, 1024))
    aps["l3f_fc3T"] = p.din("l3f_fc3T", (1024, 1024))
    aps["l3f_gateT"] = p.din("l3f_gateT", (1024, 6))
    aps["l3f_gateb"] = p.din("l3f_gateb", (1, 6))
    emit_l3_ffn(cr, aps)
    return finish_launch(p, cr, aps), aps



def emit_l2_ffn(cr, aps):
    p = cr.p
    NE, H = 16, 512
    NHS = H // 128
    wg = aps["l2f_wg"]; wu = aps["l2f_wu"]; wd = aps["l2f_wd"]
    swg = aps["l2f_swg"]; swu = aps["l2f_swu"]; swd = aps["l2f_swd"]
    gateT = aps["l2f_gateT"]

    xn_fm = p.sb("xn_fm", [128, 8 * OWN], BF16)
    ge_sb = p.sb("ge_sb", [128, NT * NE], F32)
    y_acc = p.sb("y_acc", [128, 8 * D], F32)
    h_fm = p.sb("h_fm", [128, NHS * 1024], BF16)
    w1buf = p.sb("w1buf", [128, 2 * 8 * 128], BF16)
    w3buf = p.sb("w3buf", [128, 2 * 8 * 128], BF16)
    w2buf = p.sb("w2buf", [128, NHS * 1024], BF16)
    gateT_sb = p.sb("gateT_sb", [128, 8 * NE], F32)
    gwsq = cr.scratch("gwsq", [128, 8 * NE], F32)
    gnorm = cr.scratch("gnorm", [1, 2 * NE], F32)
    gwinvb = cr.scratch("gwinvb", [128, NE], F32)
    xnt = cr.scratch("xnt", [128, D], BF16)
    xnf = cr.scratch("xnf", [128, D], F32)

    p.dma("q", gateT_sb.ap().rearrange("p (s e) -> p s e", e=NE),
          gateT.rearrange("(s p) e -> p s e", p=128), writes=[("gateT",)])

    p.op("s", lambda e: e.activation(gwsq[:], gateT_sb[:], AF.Square),
         reads=[("gateT",)], writes=[("gwsq",)], tag="gwsq")
    bank, bkey = cr.bank()
    for s_ in range(8):
        p.op("t", lambda e, s_=s_: e.matmul(
            bank[0:1, 0:NE], cr.ones_col_f[:, 0:1],
            gwsq[:, s_ * NE:(s_ + 1) * NE],
            start=(s_ == 0), stop=(s_ == 7)),
            reads=[("gwsq",), ("ones_col_f",)], writes=[bkey], tag="gwnorm_mm")
    p.op("s", lambda e: e.activation(gnorm[0:1, 0:NE], bank[0:1, 0:NE], AF.Sqrt),
         reads=[bkey], writes=[("gnorm",)])
    p.op("v", lambda e: e.tensor_scalar_max(gnorm[0:1, 0:NE], gnorm[0:1, 0:NE], 1e-12),
         reads=[("gnorm",)], writes=[("gnorm",)])
    p.op("v", lambda e: e.reciprocal(gnorm[0:1, NE:2 * NE], gnorm[0:1, 0:NE]),
         reads=[("gnorm",)], writes=[("gnorm",)])
    bank2, bkey2 = cr.bank()
    p.op("t", lambda e: e.matmul(bank2[:, 0:NE], cr.ones_row_f[0:1, 0:128],
                                 gnorm[0:1, NE:2 * NE], start=True, stop=True),
         reads=[("gnorm",), ("ones_row_f",)], writes=[bkey2], tag="gwinv_bcast")
    p.op("v", lambda e: e.tensor_copy(gwinvb[:], bank2[:, 0:NE]),
         reads=[bkey2], writes=[("gwinvb",)])

    for t in range(NT):
        x_ap = cr.x_sb[:, t * D:(t + 1) * D]
        emit_ln_tile(cr, x_ap, [("x", t)],
                     [(xnt[:], ("xnt",)), (xnf[:], ("xnf",))], sidx=t % 2)
        emit_tok_to_fm(cr, xnt[:], [("xnt",)], t, xn_fm, ("xnfm",))
        wrk = cr.scratch("l2g_wrk", [128, 8], F32)
        sq = cr.scratch("lnsq", [128, 1024], F32)
        p.op("s", lambda e: e.activation(sq[:], xnf[:], AF.Square,
                                         accum_out=wrk[:, 0:1]),
             reads=[("xnf",)], writes=[("lnsq",), ("l2gw",)], tag="xnorm_sq")
        p.op("s", lambda e: e.activation(wrk[:, 1:2], wrk[:, 0:1], AF.Sqrt),
             reads=[("l2gw",)], writes=[("l2gw",)])
        p.op("v", lambda e: e.tensor_scalar_max(wrk[:, 1:2], wrk[:, 1:2], 1e-12),
             reads=[("l2gw",)], writes=[("l2gw",)])
        p.op("v", lambda e: e.reciprocal(wrk[:, 2:3], wrk[:, 1:2]),
             reads=[("l2gw",)], writes=[("l2gw",)])
        xnff = cr.scratch("gat_xnff", [128, D], F32)
        for s_ in range(8):
            emit_tr_f32(cr, xnf[:, s_ * 128:(s_ + 1) * 128],
                        xnff[:, s_ * 128:(s_ + 1) * 128],
                        reads=[("xnf",)], writes=[("gat_xnff", s_)])
        bl, blk = cr.bank()
        for s_ in range(8):
            p.op("t", lambda e, s_=s_, bl=bl: e.matmul(
                bl[:, 0:NE], xnff[:, s_ * 128:(s_ + 1) * 128],
                gateT_sb[:, s_ * NE:(s_ + 1) * NE],
                start=(s_ == 0), stop=(s_ == 7)),
                reads=[("gat_xnff", s_), ("gateT",)], writes=[blk], tag="gat_mm")
        sg = cr.scratch("gat_s", [128, 16], F32)
        p.op("v", lambda e, bl=bl: e.tensor_scalar_mul(sg[:, 0:NE], bl[:, 0:NE],
                                                       wrk[:, 2:3]),
             reads=[blk, ("l2gw",)], writes=[("gat_s",)], tag="gat_xinv")
        p.op("v", lambda e: e.tensor_mul(sg[:, 0:NE], sg[:, 0:NE], gwinvb[:]),
             reads=[("gat_s",), ("gwinvb",)], writes=[("gat_s",)])
        p.op("s", lambda e: e.activation(sg[:, 0:NE], sg[:, 0:NE], AF.Sigmoid),
             reads=[("gat_s",)], writes=[("gat_s",)], tag="gat_sig")
        emit_topk_ge(cr, NE, 8, t, ge_sb, eps_norm=0.0)

    if os.environ.get("MK_L2_SKIP_EXPERTS"):
        for t in range(NT):
            p.op("v", lambda e, t=t: e.memset(y_acc[:, (t % 8) * D:(t % 8) * D + D], 0.0),
                 writes=[("yacc", t, 0), ("yacc", t, 1)])
        for t in range(NT):
            emit_residual_ln_add(cr, y_acc[:, (t % 8) * D:(t % 8) * D + D],
                                 [("yacc", t, 0), ("yacc", t, 1)], t, sidx=2 + (t % 2))
        return
    for tg in range(2):
        for e in range(NE + 1):
            shared = (e == NE)
            w2src = swd if shared else wd[e]
            p.dma("g", w2buf.ap().rearrange("p (s n) -> p s n", s=NHS),
                  w2src.rearrange("(s p) d -> p s d", p=128),
                  writes=[("w2buf",)], tag="w2load")
            for hs in range(NHS):
                bi = hs % 2
                w1k = ("w1buf", bi)
                w3k = ("w3buf", bi)
                src1 = (swg if shared else wg[e])
                src3 = (swu if shared else wu[e])
                dst1 = w1buf.ap().rearrange("p (b s n) -> p b s n", b=2, s=8)[:, bi]
                p.dma("g", dst1, src1.rearrange("(s p) h -> p s h", p=128)
                      [:, :, hs * 128:(hs + 1) * 128], writes=[w1k])
                dst3 = w3buf.ap().rearrange("p (b s n) -> p b s n", b=2, s=8)[:, bi]
                p.dma("g", dst3, src3.rearrange("(s p) h -> p s h", p=128)
                      [:, :, hs * 128:(hs + 1) * 128], writes=[w3k])
                w1v = w1buf.ap().rearrange("p (b n) -> p b n", b=2)[:, bi]
                w3v = w3buf.ap().rearrange("p (b n) -> p b n", b=2)[:, bi]
                emit_glu_expert(cr, xn_fm, w1v, w1k, w3v, w3k,
                                h_fm, ("hfm",), tg, hs)
            for tl in range(8):
                t = tg * 8 + tl
                for dg in range(2):
                    by, byk = cr.bank()
                    for hs in range(NHS):
                        p.op("t", lambda e2, hs=hs, tl=tl, dg=dg, by=by: e2.matmul(
                            by[:], h_fm[:, hs * 1024 + tl * 128: hs * 1024 + tl * 128 + 128],
                            w2buf[:, hs * 1024 + dg * 512: hs * 1024 + dg * 512 + 512],
                            start=(hs == 0), stop=(hs == NHS - 1)),
                            reads=[("hfm",), ("w2buf",)], writes=[byk], tag="down_mm")
                    ydst = y_acc[:, tl * D + dg * 512: tl * D + dg * 512 + 512]
                    if shared:
                        p.op("v", lambda e2, ydst=ydst, by=by: e2.tensor_add(
                            ydst, ydst, by[:]),
                            reads=[byk, ("yacc", t, dg)], writes=[("yacc", t, dg)],
                            tag="down_add")
                    elif e == 0:
                        p.op("v", lambda e2, ydst=ydst, by=by, t=t, e=e: e2.tensor_scalar_mul(
                            ydst, by[:], ge_sb[:, t * NE + e: t * NE + e + 1]),
                            reads=[byk, ("ge", t)], writes=[("yacc", t, dg)],
                            tag="down_scale0")
                    else:
                        p.op("v", lambda e2, ydst=ydst, by=by, t=t, e=e: e2.scalar_tensor_tensor(
                            ydst, by[:], ge_sb[:, t * NE + e: t * NE + e + 1],
                            ydst, ALU.mult, ALU.add),
                            reads=[byk, ("ge", t), ("yacc", t, dg)],
                            writes=[("yacc", t, dg)], tag="down_scaleacc")
        for tl in range(8):
            t = tg * 8 + tl
            emit_residual_ln_add(
                cr, y_acc[:, tl * D:(tl + 1) * D],
                [("yacc", t, 0), ("yacc", t, 1)], t, sidx=2 + (tl % 2))


def l2_ffn_host_inputs(inp):
    f32 = np.float32
    return {
        "l2f_wg": np.ascontiguousarray(inp["L2_ffn_w_gate"]).astype(f32),
        "l2f_wu": np.ascontiguousarray(inp["L2_ffn_w_up"]).astype(f32),
        "l2f_wd": np.ascontiguousarray(inp["L2_ffn_w_down"]).astype(f32),
        "l2f_swg": np.ascontiguousarray(inp["L2_ffn_sw_gate"]).astype(f32),
        "l2f_swu": np.ascontiguousarray(inp["L2_ffn_sw_up"]).astype(f32),
        "l2f_swd": np.ascontiguousarray(inp["L2_ffn_sw_down"]).astype(f32),
        "l2f_gateT": np.ascontiguousarray(inp["L2_ffn_gate_w"].T).astype(f32),
    }


def build_l2_ffn():
    p, cr, aps = begin_launch()
    aps["l2f_wg"] = p.din("l2f_wg", (16, 1024, 512))
    aps["l2f_wu"] = p.din("l2f_wu", (16, 1024, 512))
    aps["l2f_wd"] = p.din("l2f_wd", (16, 512, 1024))
    aps["l2f_swg"] = p.din("l2f_swg", (1024, 512))
    aps["l2f_swu"] = p.din("l2f_swu", (1024, 512))
    aps["l2f_swd"] = p.din("l2f_swd", (512, 1024))
    aps["l2f_gateT"] = p.din("l2f_gateT", (1024, 16))
    emit_l2_ffn(cr, aps)
    return finish_launch(p, cr, aps), aps



def load_masks(cr, consts_ap, band=False):
    p = cr.p
    cr.maskG = p.sb("maskG", [128, 1024], BF16)
    cr.maskST = p.sb("maskST", [128, 1024], BF16)
    p.dma("g", cr.maskG[:], consts_ap[129:257, 0:1024], writes=[("maskG",)])
    p.dma("g", cr.maskST[:], consts_ap[257:385, 0:1024], writes=[("maskST",)])
    if band:
        cr.maskG2 = p.sb("maskG2", [128, 1280], BF16)
        cr.maskST2 = p.sb("maskST2", [128, 1536], BF16)
        p.dma("g", cr.maskG2[:], consts_ap[385:513, 0:1280],
              writes=[("maskG2",)])
        p.dma("g", cr.maskST2[:], consts_ap[513:641, :],
              writes=[("maskST2",)])


def emit_attention(cr, cfg):
    p = cr.p
    nh, dq, dv = cfg["nh"], cfg["dq"], cfg["dv"]
    causal = cfg["causal"]
    parity = cfg["parity"]
    attn_fm = cfg["attn_fm"]
    NKT = T // 128

    qa = [p.sb(f"qaug{i}", [128, OWN], BF16) for i in range(2)]
    ka = [p.sb(f"kaug{i}", [128, T], BF16) for i in range(2)]
    vh = [p.sb(f"vh{i}", [128, NKT * dv], BF16) for i in range(2)]
    rinvb = p.sb("rinvb", [128, 4 * 512], F32)
    disc = p.sb("disc", [128, 512], BF16)
    pt_scr2 = [p.sb(f"pt_scr{i}", [128, 512], BF16) for i in range(2)]
    mstat = p.sb("mstat", [128, 64], F32)
    mrow = p.sb("mrow", [1, OWN], BF16)
    rrow = p.sb("rrow", [1, OWN], F32)
    mrow_bf = p.sb("mrow_bf", [128, 16], BF16)

    for i in range(len(ka)):
        p.op("g", lambda e, i=i: e.memset(ka[i][:], 0.0),
             writes=[("kaug_ones", i)], tag="kaug_zero")
        p.op("g", lambda e, i=i: e.memset(ka[i][dq:dq + 1, :], 1.0),
             writes=[("kaug_ones", i)], tag="kaug_ones")
        p.op("g", lambda e, i=i: e.memset(vh[i][:], 0.0),
             writes=[("vh", i)], tag="vh_zero")

    def q_abs(qt):
        return tile_pos(parity, qt)

    for h in range(nh):
        hb = h % 2
        qkey = ("qaug", hb)
        kkey = ("kaug", hb)
        vkey = ("vh", hb)
        kx = list(cfg.get("k_extra_reads", ()))
        vx = list(cfg.get("v_extra_reads", ()))
        for (apfn, row0, nrows) in cfg["q_parts"]:
            p.dma("q", qa[hb][row0:row0 + nrows, :], apfn(h),
                  writes=[qkey], tag="qload")
        kranges = cfg.get("k_ranges") or [(0, T)]
        for (apfn, row0, nrows) in cfg["k_parts"]:
            for (ra, rb) in kranges:
                p.dma("q", ka[hb][row0:row0 + nrows, ra:rb], apfn(h)[:, ra:rb],
                      writes=[kkey], reads=[("kaug_ones", hb)] + kx,
                      tag="kload")
        if cfg.get("v_tok") is not None:
            vt = cfg["v_tok"]
            for kt in range(NKT):
                if not any(ra <= kt * 128 < rb for (ra, rb) in kranges):
                    continue
                p.dma("q", vh[hb][:, kt * dv:(kt + 1) * dv],
                      vt[kt * 128:(kt + 1) * 128, h * dv:(h + 1) * dv],
                      writes=[vkey], reads=vx, tag="vload_t")
        else:
            p.dma("q", vh[hb][:], cfg["v_arr"][h], writes=[vkey], tag="vload")

        band = cfg.get("band")
        for qt in range(NT):
            if causal:
                next_ = ceil_div(q_abs(qt) + 128, 512)
            else:
                next_ = T // 512
            ks_lo = 0 if band is None else max(0, (q_abs(qt) - band) // 512)
            nchunks = ceil_div(next_ - ks_lo, 4)
            for ch in range(nchunks):
                ks0 = ks_lo + ch * 4
                ksn = min(4, next_ - ks0)
                banks = []
                for i in range(ksn):
                    ks = ks0 + i
                    bank, bkey = cr.pbank(i)
                    banks.append((bank, bkey, ks))
                    p.op("t", lambda e, bank=bank, ks=ks, qt=qt, hb=hb: e.matmul(
                        bank[:], qa[hb][0:dq, qt * 128:(qt + 1) * 128],
                        ka[hb][0:dq, ks * 512:(ks + 1) * 512],
                        start=True, stop=True),
                        reads=[qkey, kkey], writes=[bkey], tag="pa_mm")
                    if causal and band is not None:
                        k0 = ks * 512
                        dlt = q_abs(qt) - k0
                        o = 640 - dlt
                        assert 0 <= o <= 768, (qt, ks, dlt)
                        p.op("v", lambda e, bank=bank, o=o: e.tensor_add(
                            bank[:], bank[:], cr.maskG2[:, o:o + 512]),
                            reads=[bkey, ("maskG2",)], writes=[bkey],
                            tag="pa_bandmask")
                    elif causal:
                        k0 = ks * 512
                        if k0 + 512 > q_abs(qt) + 128 and k0 < q_abs(qt) + 128:
                            delta = q_abs(qt) - k0
                            assert 0 <= delta <= 384, (qt, ks, delta)
                            p.op("v", lambda e, bank=bank, delta=delta: e.tensor_add(
                                bank[:], bank[:],
                                cr.maskG[:, 384 - delta:384 - delta + 512]),
                                reads=[bkey, ("maskG",)], writes=[bkey],
                                tag="pa_mask")
                for i, (bank, bkey, ks) in enumerate(banks):
                    p.op("v", lambda e, bank=bank, i=i: e.reduce_max(
                        mstat[:, 32 + i:33 + i], bank[:], axis=AX.X),
                        reads=[bkey], writes=[("mstat",)], tag="pa_max")
                p.op("v", lambda e, ksn=ksn: e.reduce_max(
                    mstat[:, 40:41], mstat[:, 32:32 + ksn], axis=AX.X),
                    reads=[("mstat",)], writes=[("mstat",)], tag="pa_maxc")
                if ch == 0:
                    p.op("v", lambda e: e.tensor_copy(mrow_bf[:, 0:1],
                                                      mstat[:, 40:41]),
                         reads=[("mstat",)], writes=[("mrowbf",)])
                    p.op("v", lambda e, qt=qt: e.tensor_scalar_mul(
                        mstat[:, qt:qt + 1], mrow_bf[:, 0:1], -1.0),
                        reads=[("mrowbf",)], writes=[("mstat",)], tag="pa_negm")
                else:
                    p.op("v", lambda e: e.tensor_copy(mrow_bf[:, 0:1],
                                                      mstat[:, 40:41]),
                         reads=[("mstat",)], writes=[("mrowbf",)])
                    p.op("v", lambda e: e.tensor_scalar_mul(
                        mstat[:, 41:42], mrow_bf[:, 0:1], -1.0),
                        reads=[("mrowbf",)], writes=[("mstat",)])
                    p.op("v", lambda e, qt=qt: e.tensor_tensor(
                        mstat[:, 42:43], mstat[:, qt:qt + 1], mstat[:, 41:42],
                        ALU.min),
                        reads=[("mstat",)], writes=[("mstat",)], tag="pa_negmin")
                    p.op("v", lambda e, qt=qt: e.tensor_sub(
                        mstat[:, 43:44], mstat[:, 42:43], mstat[:, qt:qt + 1]),
                        reads=[("mstat",)], writes=[("mstat",)])
                    p.op("s", lambda e: e.activation(
                        mstat[:, 43:44], mstat[:, 43:44], AF.Exp),
                        reads=[("mstat",)], writes=[("mstat",)], tag="pa_scexp")
                    p.op("v", lambda e, qt=qt: e.tensor_mul(
                        mstat[:, 16 + qt:17 + qt], mstat[:, 16 + qt:17 + qt],
                        mstat[:, 43:44]),
                        reads=[("mstat",)], writes=[("mstat",)])
                    p.op("v", lambda e, qt=qt: e.tensor_copy(
                        mstat[:, qt:qt + 1], mstat[:, 42:43]),
                        reads=[("mstat",)], writes=[("mstat",)])
                for i, (bank, bkey, ks) in enumerate(banks):
                    p.op("s", lambda e, bank=bank, i=i, qt=qt: e.activation(
                        disc[:], bank[:], AF.Exp, bias=mstat[:, qt:qt + 1],
                        accum_out=mstat[:, 48 + i:49 + i]),
                        reads=[bkey, ("mstat",)], writes=[("disc",), ("mstat",)],
                        tag="pa_exp")
                p.op("v", lambda e, ksn=ksn: e.reduce_sum(
                    mstat[:, 44:45], mstat[:, 48:48 + ksn], axis=AX.X),
                    reads=[("mstat",)], writes=[("mstat",)], tag="pa_rsum")
                if ch == 0:
                    p.op("v", lambda e, qt=qt: e.tensor_copy(
                        mstat[:, 16 + qt:17 + qt], mstat[:, 44:45]),
                        reads=[("mstat",)], writes=[("mstat",)])
                else:
                    p.op("v", lambda e, qt=qt: e.tensor_add(
                        mstat[:, 16 + qt:17 + qt], mstat[:, 16 + qt:17 + qt],
                        mstat[:, 44:45]),
                        reads=[("mstat",)], writes=[("mstat",)])
        p.op("v", lambda e: e.reciprocal(mstat[:, 48:64], mstat[:, 16:32]),
             reads=[("mstat",)], writes=[("mstat",)], tag="pa_rinv")

        for qt in range(NT):
            bank, bkey = cr.pbank(4)
            p.op("t", lambda e, bank=bank, qt=qt: e.matmul(
                bank[0:1, 0:128], mstat[:, qt:qt + 1], cr.identf[:],
                start=True, stop=True, is_transpose=True),
                reads=[("mstat",), ("identf",)], writes=[bkey], tag="mrow_tr")
            p.op("v", lambda e, bank=bank, qt=qt: e.tensor_copy(
                mrow[0:1, qt * 128:(qt + 1) * 128], bank[0:1, 0:128]),
                reads=[bkey], writes=[("mrow",)], tag="mrow_cp")
            bank2, bkey2 = cr.pbank(5)
            p.op("t", lambda e, bank2=bank2, qt=qt: e.matmul(
                bank2[0:1, 0:128], mstat[:, 48 + qt:49 + qt], cr.identf[:],
                start=True, stop=True, is_transpose=True),
                reads=[("mstat",), ("identf",)], writes=[bkey2], tag="rrow_tr")
            p.op("v", lambda e, bank2=bank2, qt=qt: e.tensor_copy(
                rrow[0:1, qt * 128:(qt + 1) * 128],
                bank2[0:1, 0:128]),
                reads=[bkey2], writes=[("rrow",)], tag="rrow_cp")
        p.op("v", lambda e, hb=hb: e.tensor_copy(
            qa[hb][dq:dq + 1, :], mrow[0:1, 0:OWN]),
            reads=[("mrow",)], writes=[qkey], tag="qaug_negm")
        for qg in range(4):
            bank, bkey = cr.pbank(4 + (qg % 2))
            p.op("t", lambda e, bank=bank, qg=qg: e.matmul(
                bank[:], cr.ones_row_f[0:1, 0:128],
                rrow[0:1, qg * 512:(qg + 1) * 512],
                start=True, stop=True),
                reads=[("rrow",), ("ones_row_f",)], writes=[bkey], tag="rinv_bc")
            p.op("v", lambda e, bank=bank, qg=qg: e.tensor_copy(
                rinvb[:, qg * 512:(qg + 1) * 512], bank[:]),
                reads=[bkey], writes=[("rinvb",)], tag="rinv_cp")

        qg_abs = [q_abs(4 * g) for g in range(4)]
        av_started = [False] * 4
        maxq = [qg_abs[g] + 512 for g in range(4)]
        if cfg.get("band") is not None:
            minkt = [max(0, (qg_abs[g] - cfg["band"]) // 128) for g in range(4)]
        else:
            minkt = [0] * 4
        for kt in range(NKT):
            k0 = kt * 128
            for qg in range(4):
                if causal and k0 >= maxq[qg]:
                    continue
                if kt < minkt[qg]:
                    continue
                stb, stk = cr.pbank(4 + (kt % 2))
                p.op("t", lambda e, stb=stb, kt=kt, qg=qg, hb=hb: e.matmul(
                    stb[:], ka[hb][0:dq + 1, kt * 128:(kt + 1) * 128],
                    qa[hb][0:dq + 1, qg * 512:(qg + 1) * 512],
                    start=True, stop=True),
                    reads=[qkey, kkey, ("kaug_ones", hb)], writes=[stk],
                    tag="pb_mm")
                bd = cfg.get("band")
                if causal and bd is not None:
                    dpr = qg_abs[qg] - k0
                    o2 = 640 + dpr
                    assert 0 <= o2 <= 1023, (kt, qg, dpr)
                    p.op("v", lambda e, stb=stb, o2=o2: e.tensor_add(
                        stb[:], stb[:], cr.maskST2[:, o2:o2 + 512]),
                        reads=[stk, ("maskST2",)], writes=[stk],
                        tag="pb_bandmask")
                elif causal and qg_abs[qg] <= k0 < qg_abs[qg] + 512:
                    dpr = qg_abs[qg] - k0
                    p.op("v", lambda e, stb=stb, dpr=dpr: e.tensor_add(
                        stb[:], stb[:],
                        cr.maskST[:, 384 + dpr:384 + dpr + 512]),
                        reads=[stk, ("maskST",)], writes=[stk],
                        tag="pb_mask")
                pti = (kt * 4 + qg) % 2
                pt_scr = pt_scr2[pti]
                ptk = ("ptscr", pti)
                p.op("s", lambda e, stb=stb, pt_scr=pt_scr: e.activation(
                    pt_scr[:], stb[:], AF.Exp),
                    reads=[stk], writes=[ptk], tag="pb_exp")
                avb, avk = cr.pbank(qg % 4)
                first = not av_started[qg]
                av_started[qg] = True
                lastkt = NKT - 1 if not causal else \
                    min(NKT - 1, (maxq[qg] - 1) // 128)
                p.op("t", lambda e, avb=avb, kt=kt, hb=hb, first=first,
                     pt_scr=pt_scr,
                     last=(kt == lastkt): e.matmul(
                         avb[0:dv, :], vh[hb][:, kt * dv:(kt + 1) * dv],
                         pt_scr[:], start=first, stop=last),
                     reads=[ptk, vkey], writes=[avk], tag="pb_av")
        for qg in range(4):
            avb, avk = cr.pbank(qg % 4)
            slab = (h * dv) // 128
            r0 = (h * dv) % 128
            dst = attn_fm.ap().rearrange("p (s n) -> p s n", n=OWN)[
                r0:r0 + dv, slab, qg * 512:(qg + 1) * 512]
            p.op("v", lambda e, avb=avb, qg=qg, dst=dst: e.scalar_tensor_tensor(
                dst, avb[0:dv, :], 1.0, rinvb[0:dv, qg * 512:(qg + 1) * 512],
                ALU.mult, ALU.mult),
                reads=[avk, ("rinvb",)], writes=[("attnfm",)], tag="av_evac")
        av_started = [False] * 4



def emit_rope_interleaved(cr, src_f32, srckey, dst_bf, dstkey, cos_sb, sin_sb,
                          tbl_key, t):
    p = cr.p
    rs = cr.scratch("rope_scr", [128, 1024], F32)
    rkey = ("rope_scr",)
    C = cos_sb[:, 0:512]
    S = sin_sb[:, 0:512]
    se = src_f32.rearrange("p (c two) -> p c two", two=2)[:, :, 0]
    so = src_f32.rearrange("p (c two) -> p c two", two=2)[:, :, 1]
    de = dst_bf.rearrange("p (c two) -> p c two", two=2)[:, :, 0]
    do = dst_bf.rearrange("p (c two) -> p c two", two=2)[:, :, 1]
    p.op("v", lambda e: e.tensor_mul(rs[:, 0:512], se, C),
         reads=[srckey, tbl_key], writes=[rkey], tag="rope1")
    p.op("v", lambda e: e.tensor_mul(rs[:, 512:1024], so, S),
         reads=[srckey, tbl_key], writes=[rkey], tag="rope2")
    p.op("v", lambda e: e.tensor_sub(de, rs[:, 0:512], rs[:, 512:1024]),
         reads=[rkey], writes=[dstkey], tag="rope3")
    p.op("v", lambda e: e.tensor_mul(rs[:, 0:512], se, S),
         reads=[srckey, tbl_key], writes=[rkey], tag="rope4")
    p.op("v", lambda e: e.tensor_mul(rs[:, 512:1024], so, C),
         reads=[srckey, tbl_key], writes=[rkey], tag="rope5")
    p.op("v", lambda e: e.tensor_add(do, rs[:, 0:512], rs[:, 512:1024]),
         reads=[rkey], writes=[dstkey], tag="rope6")


def emit_proj_to_tok(cr, xn_fm, w_sb, wkey, t, nslab_in, out_w, bias_sb=None,
                     bkey=None, out_cb=None, tag="proj"):
    p = cr.p
    for dg in range(ceil_div(out_w, 512)):
        gw = min(512, out_w - dg * 512)
        bank, bk = cr.bank()
        for s_ in range(nslab_in):
            p.op("t", lambda e, s_=s_, dg=dg, bank=bank, gw=gw: e.matmul(
                bank[:, 0:gw],
                xn_fm.ap().rearrange("p (s n) -> p s n", n=OWN)[
                    :, s_, t * 128:(t + 1) * 128],
                w_sb[:, s_ * out_w + dg * 512: s_ * out_w + dg * 512 + gw],
                start=(s_ == 0), stop=(s_ == nslab_in - 1 and bias_sb is None)),
                reads=[("xnfm",), wkey], writes=[bk], tag=tag)
        if bias_sb is not None:
            p.op("t", lambda e, dg=dg, bank=bank, gw=gw: e.matmul(
                bank[:, 0:gw], cr.ones_row[0:1, 0:128],
                bias_sb[0:1, dg * 512: dg * 512 + gw],
                start=False, stop=True),
                reads=[bkey, ("ones_row",)], writes=[bk], tag=tag + "_b")
        out_cb(dg, bank, bk)


def l0_attn_host_inputs(inp):
    f32 = np.float32
    d = {
        "l0a_wqT": np.ascontiguousarray(inp["L0_attn_W_q"].T).astype(f32),
        "l0a_wkT": np.ascontiguousarray(inp["L0_attn_W_k"].T).astype(f32),
        "l0a_wvT": np.ascontiguousarray(inp["L0_attn_W_v"].T).astype(f32),
        "l0a_woT": np.ascontiguousarray(inp["L0_attn_W_o"].T).astype(f32),
    }
    j = np.arange(32, dtype=np.float64)
    inv = 10000.0 ** (-2.0 * j / 64)
    for c in range(NCORES):
        pos = own_index(c % 2).astype(np.float64)
        ang = pos[:, None] * inv[None, :]
        cos = np.cos(ang)
        sin = np.sin(ang)
        cosr = np.tile(cos, (1, 16)).astype(f32)
        sinr = np.tile(sin, (1, 16)).astype(f32)
        d.setdefault("l0a_cos", {})[c] = cosr
        d.setdefault("l0a_sin", {})[c] = sinr
    d["l0a_cos"] = (lambda m: (lambda c: m[c]))(d["l0a_cos"])
    d["l0a_sin"] = (lambda m: (lambda c: m[c]))(d["l0a_sin"])
    return d


def build_l0_attn_a():
    p, cr, aps = begin_launch()
    wq = p.din("l0a_wqT", (1024, 1024))
    wk = p.din("l0a_wkT", (1024, 1024))
    wv = p.din("l0a_wvT", (1024, 1024))
    cosd = p.din("l0a_cos", (OWN, 512))
    sind = p.din("l0a_sin", (OWN, 512))
    q_out = p.dout("q_out", (OWN, 1024), BF16)
    k_out = p.dout("k_out", (OWN, 1024), BF16)
    v_out = p.dout("v_out", (OWN, 1024), BF16)

    xn_fm = p.sb("xn_fm", [128, 8 * OWN], BF16)
    xnt = cr.scratch("xnt", [128, D], BF16)
    wq_sb = p.sb("wq_sb", [128, 8 * 1024], BF16)
    wk_sb = p.sb("wk_sb", [128, 8 * 1024], BF16)
    wv_sb = p.sb("wv_sb", [128, 8 * 1024], BF16)
    cos_sb = p.sb("cos_sb", [128, 512], F32)
    sin_sb = p.sb("sin_sb", [128, 512], F32)
    qtf = cr.scratch("qtf", [128, 1024], F32)
    qtb = cr.scratch("qtb", [128, 1024], BF16)

    for nm, w, sb_ in (("wq", wq, wq_sb), ("wk", wk, wk_sb), ("wv", wv, wv_sb)):
        p.dma("g", sb_.ap().rearrange("p (s n) -> p s n", s=8),
              w.rearrange("(s p) n -> p s n", p=128), writes=[(nm,)], tag=nm)


    for t in range(NT):
        x_ap = cr.x_sb[:, t * D:(t + 1) * D]
        emit_ln_tile(cr, x_ap, [("x", t)], [(xnt[:], ("xnt",))], sidx=t % 2)
        emit_tok_to_fm(cr, xnt[:], [("xnt",)], t, xn_fm, ("xnfm",))

    for t in range(NT):
        for nm, wsb, wkey2, outd, rope, tbls in (
                ("q", wq_sb, ("wq",), q_out, True, ("q",)),
                ("k", wk_sb, ("wk",), k_out, True, ("k",)),
                ("v", wv_sb, ("wv",), v_out, False, None)):
            def cb(dg, bank, bk, nm=nm, t=t):
                p.op("v", lambda e, dg=dg, bank=bank: e.tensor_copy(
                    qtf[:, dg * 512:(dg + 1) * 512], bank[:]),
                    reads=[bk], writes=[("qtf",)], tag="pj_evac")
            emit_proj_to_tok(cr, xn_fm, wsb, wkey2, t, 8, 1024, out_cb=cb,
                             tag=f"{nm}mm")
            if rope:
                if nm == "q":
                    p.dma("q", cos_sb[:],
                          cosd.rearrange("(t p) n -> p t n", p=128)[:, t],
                          writes=[("ctbl",)], tag="ctbl")
                    p.dma("q", sin_sb[:],
                          sind.rearrange("(t p) n -> p t n", p=128)[:, t],
                          writes=[("ctbl",)], tag="stbl")
                emit_rope_interleaved(cr, qtf.ap(), ("qtf",), qtb.ap(),
                                      ("qtb",), cos_sb, sin_sb,
                                      ("ctbl",), t)
            else:
                p.op("v", lambda e: e.tensor_copy(qtb[:], qtf[:]),
                     reads=[("qtf",)], writes=[("qtb",)], tag="v_cast")
            if nm == "q":
                p.op("v", lambda e: e.tensor_scalar_mul(qtb[:], qtb[:], 0.125),
                     reads=[("qtb",)], writes=[("qtb",)], tag="q_scale")
            p.dma("q", outd[t * 128:(t + 1) * 128, :], qtb[:],
                  reads=[("qtb",)], tag="qkv_store")
    return finish_launch(p, cr, aps), aps


def build_l0_attn_b(parity):
    p, cr, aps = begin_launch()
    load_masks(cr, aps["consts"])
    q_fm = p.din("q_fm", (8, 128, OWN), BF16)
    k_fm = p.din("k_fm", (8, 128, T), BF16)
    v_arr = p.din("v_arr", (16, 128, (T // 128) * 64), BF16)
    wo = p.din("l0a_woT", (1024, 1024))

    attn_fm = p.sb("attn_fm", [128, 8 * OWN], BF16)
    wo_sb = p.sb("wo_sb", [128, 8 * 1024], BF16)
    yt = cr.scratch("yt", [128, 1024], F32)
    p.dma("g", wo_sb.ap().rearrange("p (s n) -> p s n", s=8),
          wo.rearrange("(s p) n -> p s n", p=128), writes=[("wo",)])

    cfg = dict(
        nh=16, dq=64, dv=64, causal=True, parity=parity,
        q_parts=[(lambda h: q_fm[h // 2, (h % 2) * 64:(h % 2) * 64 + 64, :],
                  0, 64)],
        k_parts=[(lambda h: k_fm[h // 2, (h % 2) * 64:(h % 2) * 64 + 64, :],
                  0, 64)],
        v_arr=v_arr, attn_fm=attn_fm)
    emit_attention(cr, cfg)

    for t in range(NT):
        def cb(dg, bank, bk, t=t):
            p.op("v", lambda e, dg=dg, bank=bank: e.tensor_copy(
                yt[:, dg * 512:(dg + 1) * 512], bank[:]),
                reads=[bk], writes=[("yt",)], tag="wo_evac")
        for dg in range(2):
            bank, bk = cr.bank()
            for s_ in range(8):
                p.op("t", lambda e, s_=s_, dg=dg, bank=bank, t=t: e.matmul(
                    bank[:],
                    attn_fm.ap().rearrange("p (s n) -> p s n", n=OWN)[
                        :, s_, t * 128:(t + 1) * 128],
                    wo_sb[:, s_ * 1024 + dg * 512: s_ * 1024 + dg * 512 + 512],
                    start=(s_ == 0), stop=(s_ == 7)),
                    reads=[("attnfm",), ("wo",)], writes=[bk], tag="wo_mm")
            cb(dg, bank, bk)
        emit_residual_ln_add(cr, yt.ap(), [("yt",)], t, sidx=2)
    return finish_launch(p, cr, aps), aps


def host_fm_rearrange(q_tok):
    n = q_tok.shape[0]
    return np.ascontiguousarray(
        q_tok.reshape(n, 8, 128).transpose(1, 2, 0))


def host_v_arrange(v_full):
    nkt = v_full.shape[0] // 128
    v4 = v_full.reshape(nkt, 128, 16, 64)
    return np.ascontiguousarray(v4.transpose(2, 1, 0, 3).reshape(16, 128, nkt * 64))


def run_l0_attn(x_shards, inp, run):
    consts = np_consts()
    extra = l0_attn_host_inputs(inp)
    nc_a, _ = build_l0_attn_a()
    in_maps = []
    for c in range(NCORES):
        m = dict(x_in=x_shards[c], consts=consts,
                 l0a_wqT=extra["l0a_wqT"], l0a_wkT=extra["l0a_wkT"],
                 l0a_wvT=extra["l0a_wvT"],
                 l0a_cos=extra["l0a_cos"](c), l0a_sin=extra["l0a_sin"](c))
        in_maps.append(m)
    resA = run(nc_a, in_maps)
    k_full = {}
    v_full = {}
    for b_ in range(B):
        kf = np.zeros((T, 1024), dtype=BF)
        vf = np.zeros((T, 1024), dtype=BF)
        for par in range(2):
            c = 2 * b_ + par
            kf[own_index(par)] = resA[c]["k_out"]
            vf[own_index(par)] = resA[c]["v_out"]
        k_full[b_] = host_fm_rearrange(kf)
        v_full[b_] = host_v_arrange(vf)
    nc_b = {}
    for par in range(2):
        nc_b[par], _ = build_l0_attn_b(par)
    out_shards = [None] * NCORES
    for par in range(2):
        in_maps = []
        cores = [2 * b_ + par for b_ in range(B)]
        for b_ in range(B):
            c = 2 * b_ + par
            in_maps.append(dict(
                x_in=x_shards[c], consts=consts,
                q_fm=host_fm_rearrange(resA[c]["q_out"]),
                k_fm=k_full[b_], v_arr=v_full[b_],
                l0a_woT=extra["l0a_woT"]))
        resB = run(nc_b[par], in_maps, core_ids=cores)
        for i, c in enumerate(cores):
            out_shards[c] = resB[i]["x_out"]
    return out_shards



def emit_mm_combine(cr, aT_dram, b_dram, kdim, n_out, out_sb, outkey, tag):
    p = cr.p
    nks = kdim // 128
    ndg = ceil_div(n_out, 512)
    at_sb = p.sb(f"{tag}_at", [128, 2 * 128], BF16)
    b_sb = p.sb(f"{tag}_b", [128, 2 * n_out], BF16)
    for it in range(8):
        banks = [cr.bank() for _ in range(ndg)]
        for ks in range(nks):
            bi = ks % 2
            p.dma("g", at_sb.ap().rearrange("p (b n) -> p b n", b=2)[:, bi],
                  aT_dram.rearrange("(s p) n -> p s n", p=128)
                  [:, ks, it * 128:(it + 1) * 128],
                  writes=[(tag, "at", bi)], tag=tag + "_lat")
            p.dma("g", b_sb.ap().rearrange("p (b n) -> p b n", b=2)[:, bi],
                  b_dram.rearrange("(s p) n -> p s n", p=128)[:, ks, :],
                  writes=[(tag, "b", bi)], tag=tag + "_lb")
            for dg in range(ndg):
                gw = min(512, n_out - dg * 512)
                bank, bk = banks[dg]
                p.op("t", lambda e, ks=ks, bi=bi, dg=dg, bank=bank, gw=gw: e.matmul(
                    bank[:, 0:gw],
                    at_sb.ap().rearrange("p (b n) -> p b n", b=2)[:, bi],
                    b_sb.ap().rearrange("p (b n) -> p b n", b=2)
                    [:, bi, dg * 512:dg * 512 + gw],
                    start=(ks == 0), stop=(ks == nks - 1)),
                    reads=[(tag, "at", bi), (tag, "b", bi)], writes=[bk],
                    tag=tag + "_mm")
        for dg in range(ndg):
            gw = min(512, n_out - dg * 512)
            bank, bk = banks[dg]
            p.op("v", lambda e, it=it, dg=dg, bank=bank, gw=gw: e.tensor_copy(
                out_sb[:, it * n_out + dg * 512: it * n_out + dg * 512 + gw],
                bank[:, 0:gw]),
                reads=[bk], writes=[outkey], tag=tag + "_ev")


def l1_attn_host_inputs(inp):
    f32 = np.float32
    d = {
        "l1a_wdqT": np.ascontiguousarray(inp["L1_attn_W_DQ"].T).astype(f32),
        "l1a_wuq": np.ascontiguousarray(inp["L1_attn_W_UQ"]).astype(f32),
        "l1a_wqr": np.ascontiguousarray(inp["L1_attn_W_QR"]).astype(f32),
        "l1a_wdkv": np.ascontiguousarray(inp["L1_attn_W_DKV"]).astype(f32),
        "l1a_wkr": np.ascontiguousarray(inp["L1_attn_W_KR"]).astype(f32),
        "l1a_wuk": np.ascontiguousarray(inp["L1_attn_W_UK"]).astype(f32),
        "l1a_wuv": np.ascontiguousarray(inp["L1_attn_W_UV"]).astype(f32),
        "l1a_woT": np.ascontiguousarray(inp["L1_attn_W_O"]).astype(f32),
    }
    jj = np.arange(0, 32, 2, dtype=np.float64)
    invf = 1.0 / 10000.0 ** (jj / 32)
    tpos = np.arange(T, dtype=np.float64)
    fr = tpos[:, None] * invf[None, :]
    emb = np.concatenate([fr, fr], axis=-1)
    scale = (np.cos(emb) + np.sin(emb))
    sc96 = 1.0 / np.sqrt(96.0)
    for c in range(NCORES):
        pos = own_index(c % 2)
        qs = np.tile(scale[pos], (1, 16)).astype(f32) * sc96
        ks = scale[pos].astype(f32)
        d.setdefault("_qs", {})[c] = np.ascontiguousarray(qs)
        d.setdefault("_ks", {})[c] = np.ascontiguousarray(ks)
    d["l1a_qs"] = (lambda m: (lambda c: m[c]))(d.pop("_qs"))
    d["l1a_ks"] = (lambda m: (lambda c: m[c]))(d.pop("_ks"))
    return d


def build_l1_attn_a():
    p, cr, aps = begin_launch()
    wdqT = p.din("l1a_wdqT", (4096, 1024))
    wuq = p.din("l1a_wuq", (4096, 1024))
    wqr = p.din("l1a_wqr", (4096, 512))
    wdkv = p.din("l1a_wdkv", (1024, 256))
    wkr = p.din("l1a_wkr", (1024, 32))
    qs_t = p.din("l1a_qs", (OWN, 512))
    ks_t = p.din("l1a_ks", (OWN, 32))
    qc_out = p.dout("qc_out", (OWN, 1024), BF16)
    qr_out = p.dout("qr_out", (OWN, 512), BF16)
    ckv_out = p.dout("ckv_out", (OWN, 256), BF16)
    kr_out = p.dout("kr_out", (OWN, 32), BF16)

    xn_fm = p.sb("xn_fm", [128, 8 * OWN], BF16)
    xnt = cr.scratch("xnt", [128, D], BF16)
    wqc_sb = p.sb("wqc_sb", [128, 8 * 1024], BF16)
    wqr_sb = p.sb("wqr_sb", [128, 8 * 512], BF16)
    wsm_sb = p.sb("wsm_sb", [128, 8 * (256 + 32)], BF16)
    qtf = cr.scratch("qtf", [128, 1024], F32)
    qtb = cr.scratch("qtb", [128, 1024], BF16)
    tbl = cr.scratch("tbl", [128, 512], F32)

    emit_mm_combine(cr, wdqT, wuq, 4096, 1024, wqc_sb, ("wqc",), "cmbq")
    emit_mm_combine(cr, wdqT, wqr, 4096, 512, wqr_sb, ("wqr",), "cmbr")
    p.dma("g", wsm_sb.ap().rearrange("p (s n) -> p s n", s=8)[:, :, 0:256],
          wdkv.rearrange("(s p) n -> p s n", p=128), writes=[("wsm",)])
    p.dma("g", wsm_sb.ap().rearrange("p (s n) -> p s n", s=8)[:, :, 256:288],
          wkr.rearrange("(s p) n -> p s n", p=128), writes=[("wsm",)])

    for t in range(NT):
        x_ap = cr.x_sb[:, t * D:(t + 1) * D]
        emit_ln_tile(cr, x_ap, [("x", t)], [(xnt[:], ("xnt",))], sidx=t % 2)
        emit_tok_to_fm(cr, xnt[:], [("xnt",)], t, xn_fm, ("xnfm",))

    for t in range(NT):
        def evac(dg, bank, bk):
            p.op("v", lambda e, dg=dg, bank=bank: e.tensor_copy(
                qtf[:, dg * 512:(dg + 1) * 512], bank[:]),
                reads=[bk], writes=[("qtf",)], tag="pj_evac")
        emit_proj_to_tok(cr, xn_fm, wqc_sb, ("wqc",), t, 8, 1024, out_cb=evac,
                         tag="qc_mm")
        p.op("v", lambda e: e.tensor_scalar_mul(qtb[:], qtf[:],
                                                float(1.0 / np.sqrt(96.0))),
             reads=[("qtf",)], writes=[("qtb",)], tag="qc_scale")
        p.dma("q", qc_out[t * 128:(t + 1) * 128, :], qtb[:],
              reads=[("qtb",)], tag="qc_store")
        emit_proj_to_tok(cr, xn_fm, wqr_sb, ("wqr",), t, 8, 512, out_cb=evac,
                         tag="qr_mm")
        p.dma("q", tbl[:], qs_t.rearrange("(t p) n -> p t n", p=128)[:, t],
              writes=[("tbl",)], tag="qs_load")
        p.op("v", lambda e: e.tensor_mul(qtb[:, 0:512], qtf[:, 0:512], tbl[:]),
             reads=[("qtf",), ("tbl",)], writes=[("qtb",)], tag="qr_scale")
        p.dma("q", qr_out[t * 128:(t + 1) * 128, :], qtb[:, 0:512],
              reads=[("qtb",)], tag="qr_store")
        emit_proj_to_tok(cr, xn_fm, wsm_sb, ("wsm",), t, 8, 288, out_cb=evac,
                         tag="kv_mm")
        p.op("v", lambda e: e.tensor_copy(qtb[:, 0:256], qtf[:, 0:256]),
             reads=[("qtf",)], writes=[("qtb",)], tag="ckv_cast")
        p.dma("q", tbl[:, 0:32], ks_t.rearrange("(t p) n -> p t n", p=128)[:, t],
              writes=[("tbl",)], tag="ks_load")
        p.op("v", lambda e: e.tensor_mul(qtb[:, 256:288], qtf[:, 256:288],
                                         tbl[:, 0:32]),
             reads=[("qtf",), ("tbl",)], writes=[("qtb",)], tag="kr_scale")
        p.dma("q", ckv_out[t * 128:(t + 1) * 128, :], qtb[:, 0:256],
              reads=[("qtb",)], tag="ckv_store")
        p.dma("q", kr_out[t * 128:(t + 1) * 128, :], qtb[:, 256:288],
              reads=[("qtb",)], tag="kr_store")
    return finish_launch(p, cr, aps), aps


def build_l1_attn_b(parity):
    p, cr, aps = begin_launch()
    load_masks(cr, aps["consts"])
    qc_fm = p.din("qc_fm", (8, 128, OWN), BF16)
    qr_fm = p.din("qr_fm", (4, 128, OWN), BF16)
    ckv_fm = p.din("ckv_fm", (2, 128, T), BF16)
    krT = p.din("krT", (32, T), BF16)
    wuk = p.din("l1a_wuk", (256, 1024))
    wuv = p.din("l1a_wuv", (256, 1024))
    wo = p.din("l1a_woT", (1024, 1024))

    kc_scr = p.dtmp("kc_scr", (8, 128, T), BF16)
    v_scr = p.dtmp("v_scr", (T, 1024), BF16)

    ckv_sb = p.sb("ckv_sb", [128, 2 * 2 * 512], BF16)
    wuk_sb = p.sb("wuk_sb", [128, 2 * 1024], BF16)
    wuv_sb = p.sb("wuv_sb", [128, 2 * 1024], BF16)
    kvt = cr.scratch("kvt", [128, 1024], BF16)
    attn_fm = p.sb("attn_fm", [128, 8 * OWN], BF16)
    wo_sb = p.sb("wo_sb", [128, 8 * 1024], BF16)
    yt = cr.scratch("yt", [128, 1024], F32)


    p.dma("g", wuk_sb.ap().rearrange("p (s n) -> p s n", s=2),
          wuk.rearrange("(s p) n -> p s n", p=128), writes=[("wuk",)])
    p.dma("g", wuv_sb.ap().rearrange("p (s n) -> p s n", s=2),
          wuv.rearrange("(s p) n -> p s n", p=128), writes=[("wuv",)])
    p.dma("g", wo_sb.ap().rearrange("p (s n) -> p s n", s=8),
          wo.rearrange("(s p) n -> p s n", p=128), writes=[("wo",)])

    for tg in range(T // 512):
        bi = tg % 2
        ckv_v = ckv_sb.ap().rearrange("p (b s n) -> p b s n", b=2, s=2)[:, bi]
        p.dma("q", ckv_v, ckv_fm[:, :, tg * 512:(tg + 1) * 512]
              .rearrange("s p n -> p s n"), writes=[("ckv", bi)], tag="ckv_ld")
        for ds in range(8):
            bank, bk = cr.bank()
            for s_ in range(2):
                p.op("t", lambda e, ds=ds, s_=s_, bank=bank, ckv_v=ckv_v,
                     bi=bi: e.matmul(
                         bank[:],
                         wuk_sb[:, s_ * 1024 + ds * 128: s_ * 1024 + (ds + 1) * 128],
                         ckv_v[:, s_, :],
                         start=(s_ == 0), stop=(s_ == 1)),
                     reads=[("wuk",), ("ckv", bi)], writes=[bk], tag="kc_mm")
            p.op("v", lambda e, bank=bank: e.tensor_copy(kvt[:, 0:512], bank[:]),
                 reads=[bk], writes=[("kvt",)], tag="kc_ev")
            p.dma("q", kc_scr[ds, :, tg * 512:(tg + 1) * 512], kvt[:, 0:512],
                  reads=[("kvt",)], writes=[("kcscr",)], tag="kc_store")
        for ktl in range(4):
            kt = tg * 4 + ktl
            for dg in range(2):
                bank, bk = cr.bank()
                for s_ in range(2):
                    p.op("t", lambda e, ktl=ktl, dg=dg, s_=s_, bank=bank,
                         ckv_v=ckv_v, bi=bi: e.matmul(
                             bank[:],
                             ckv_v[:, s_, ktl * 128:(ktl + 1) * 128],
                             wuv_sb[:, s_ * 1024 + dg * 512: s_ * 1024 + dg * 512 + 512],
                             start=(s_ == 0), stop=(s_ == 1)),
                         reads=[("wuv",), ("ckv", bi)], writes=[bk], tag="v_mm")
                p.op("v", lambda e, dg=dg, bank=bank: e.tensor_copy(
                    kvt[:, dg * 512:(dg + 1) * 512], bank[:]),
                    reads=[bk], writes=[("kvt",)], tag="v_ev")
            p.dma("q", v_scr[kt * 128:(kt + 1) * 128, :], kvt[:],
                  reads=[("kvt",)], writes=[("vscr",)], tag="v_store")

    cfg = dict(
        nh=16, dq=96, dv=64, causal=True, parity=parity,
        q_parts=[
            (lambda h: qc_fm[h // 2, (h % 2) * 64:(h % 2) * 64 + 64, :], 0, 64),
            (lambda h: qr_fm[h // 4, (h % 4) * 32:(h % 4) * 32 + 32, :], 64, 32)],
        k_parts=[
            (lambda h: kc_scr[h // 2, (h % 2) * 64:(h % 2) * 64 + 64, :], 0, 64),
            (lambda h: krT, 64, 32)],
        k_extra_reads=[("kcscr",)],
        v_tok=v_scr, v_extra_reads=[("vscr",)],
        v_arr=None, attn_fm=attn_fm)
    emit_attention(cr, cfg)

    for t in range(NT):
        for dg in range(2):
            bank, bk = cr.bank()
            for s_ in range(8):
                p.op("t", lambda e, s_=s_, dg=dg, bank=bank, t=t: e.matmul(
                    bank[:],
                    attn_fm.ap().rearrange("p (s n) -> p s n", n=OWN)[
                        :, s_, t * 128:(t + 1) * 128],
                    wo_sb[:, s_ * 1024 + dg * 512: s_ * 1024 + dg * 512 + 512],
                    start=(s_ == 0), stop=(s_ == 7)),
                    reads=[("attnfm",), ("wo",)], writes=[bk], tag="wo_mm")
            p.op("v", lambda e, dg=dg, bank=bank: e.tensor_copy(
                yt[:, dg * 512:(dg + 1) * 512], bank[:]),
                reads=[bk], writes=[("yt",)], tag="wo_evac")
        emit_residual_ln_add(cr, yt.ap(), [("yt",)], t, sidx=2)
    return finish_launch(p, cr, aps), aps


def run_l1_attn(x_shards, inp, run):
    consts = np_consts()
    extra = l1_attn_host_inputs(inp)
    nc_a, _ = build_l1_attn_a()
    in_maps = []
    for c in range(NCORES):
        m = dict(x_in=x_shards[c], consts=consts,
                 l1a_qs=extra["l1a_qs"](c), l1a_ks=extra["l1a_ks"](c))
        for k in ("l1a_wdqT", "l1a_wuq", "l1a_wqr", "l1a_wdkv", "l1a_wkr"):
            m[k] = extra[k]
        in_maps.append(m)
    resA = run(nc_a, in_maps)
    ckv_full = {}
    kr_full = {}
    for b_ in range(B):
        cf = np.zeros((T, 256), dtype=BF)
        kf = np.zeros((T, 32), dtype=BF)
        for par in range(2):
            c = 2 * b_ + par
            cf[own_index(par)] = resA[c]["ckv_out"]
            kf[own_index(par)] = resA[c]["kr_out"]
        ckv_full[b_] = np.ascontiguousarray(
            cf.reshape(T, 2, 128).transpose(1, 2, 0))
        kr_full[b_] = np.ascontiguousarray(kf.T)
    nc_b = {}
    for par in range(2):
        nc_b[par], _ = build_l1_attn_b(par)
    out_shards = [None] * NCORES
    for par in range(2):
        cores = [2 * b_ + par for b_ in range(B)]
        in_maps = []
        for b_ in range(B):
            c = 2 * b_ + par
            qcf = host_fm_rearrange(resA[c]["qc_out"])
            qrf = np.ascontiguousarray(
                resA[c]["qr_out"].reshape(OWN, 4, 128).transpose(1, 2, 0))
            in_maps.append(dict(
                x_in=x_shards[c], consts=consts,
                qc_fm=qcf, qr_fm=qrf, ckv_fm=ckv_full[b_], krT=kr_full[b_],
                l1a_wuk=extra["l1a_wuk"], l1a_wuv=extra["l1a_wuv"],
                l1a_woT=extra["l1a_woT"]))
        resB = run(nc_b[par], in_maps, core_ids=cores)
        for i, c in enumerate(cores):
            out_shards[c] = resB[i]["x_out"]
    return out_shards



def emit_rope_halfsplit(cr, src_f32, srckey, dst_bf, dstkey, cos_sb, sin_sb,
                        tbl_key, s1, s2):
    p = cr.p
    rs = cr.scratch("rope_scr", [128, 1024], F32)
    rkey = ("rope_scr",)
    v3 = src_f32.rearrange("p (h j) -> p h j", j=64)
    x1 = v3[:, :, 0:32]
    x2 = v3[:, :, 32:64]
    d3 = dst_bf.rearrange("p (h j) -> p h j", j=64)
    d1 = d3[:, :, 0:32]
    d2 = d3[:, :, 32:64]
    C = cos_sb[:, 0:512].rearrange("p (h j) -> p h j", j=32)
    S = sin_sb[:, 0:512].rearrange("p (h j) -> p h j", j=32)
    r1 = rs.ap().rearrange("p (k c) -> p k c", k=2)[:, 0].rearrange(
        "p (h j) -> p h j", j=32)
    r2 = rs.ap().rearrange("p (k c) -> p k c", k=2)[:, 1].rearrange(
        "p (h j) -> p h j", j=32)
    p.op("v", lambda e: e.tensor_mul(r1, x1, C), reads=[srckey, tbl_key],
         writes=[rkey], tag="rh1")
    p.op("v", lambda e: e.tensor_mul(r2, x2, S), reads=[srckey, tbl_key],
         writes=[rkey], tag="rh2")
    p.op("v", lambda e: (e.tensor_add if s1 > 0 else e.tensor_sub)(d1, r1, r2),
         reads=[rkey], writes=[dstkey], tag="rh3")
    p.op("v", lambda e: e.tensor_mul(r1, x2, C), reads=[srckey, tbl_key],
         writes=[rkey], tag="rh4")
    p.op("v", lambda e: e.tensor_mul(r2, x1, S), reads=[srckey, tbl_key],
         writes=[rkey], tag="rh5")
    p.op("v", lambda e: (e.tensor_add if s2 > 0 else e.tensor_sub)(d2, r1, r2),
         reads=[rkey], writes=[dstkey], tag="rh6")


def l1_ffn_host_inputs(inp):
    f32 = np.float32
    d = {}
    for nm in ("Wq", "Wk", "Wv", "Wo", "Wg"):
        d[f"l1f_{nm}"] = np.ascontiguousarray(inp[f"L1_ffn_{nm}"]).astype(f32)
    for nm in ("bq", "bk", "bv", "bo", "bg", "g1", "be1", "g2", "be2"):
        d[f"l1f_{nm}"] = inp[f"L1_ffn_{nm}"].reshape(1, -1).astype(f32)
    d["l1f_W1"] = np.ascontiguousarray(inp["L1_ffn_W1"]).astype(f32)
    d["l1f_W2"] = np.ascontiguousarray(inp["L1_ffn_W2"]).astype(f32)
    d["l1f_b1T"] = np.ascontiguousarray(
        inp["L1_ffn_b1"].reshape(8, 32, 128).transpose(0, 2, 1)).astype(f32)
    d["l1f_b2"] = np.ascontiguousarray(inp["L1_ffn_b2"]).astype(f32)
    jj = np.arange(0, 64, 2, dtype=np.float64)
    invf = 1.0 / 10000.0 ** (jj / 64)
    tpos = np.arange(T, dtype=np.float64)
    fr = tpos[:, None] * invf[None, :]
    cos = np.cos(fr)
    sin = np.sin(fr)
    for c in range(NCORES):
        pos = own_index(c % 2)
        d.setdefault("_c", {})[c] = np.ascontiguousarray(
            np.tile(cos[pos], (1, 16)).astype(f32))
        d.setdefault("_s", {})[c] = np.ascontiguousarray(
            np.tile(sin[pos], (1, 16)).astype(f32))
    d["l1f_cos"] = (lambda m: (lambda c: m[c]))(d.pop("_c"))
    d["l1f_sin"] = (lambda m: (lambda c: m[c]))(d.pop("_s"))
    return d


def build_l1_ffn_a():
    p, cr, aps = begin_launch()
    wq = p.din("l1f_Wq", (1024, 1024))
    wk = p.din("l1f_Wk", (1024, 1024))
    wv = p.din("l1f_Wv", (1024, 1024))
    bq = p.din("l1f_bq", (1, 1024))
    bk = p.din("l1f_bk", (1, 1024))
    bv = p.din("l1f_bv", (1, 1024))
    cosd = p.din("l1f_cos", (OWN, 512))
    sind = p.din("l1f_sin", (OWN, 512))
    q_out = p.dout("q_out", (OWN, 1024), BF16)
    k_out = p.dout("k_out", (OWN, 1024), BF16)
    v_out = p.dout("v_out", (OWN, 1024), BF16)

    xn_fm = p.sb("xn_fm", [128, 8 * OWN], BF16)
    xnt = cr.scratch("xnt", [128, D], BF16)
    wsb = {}
    bsb = {}
    for nm, w, b in (("q", wq, bq), ("k", wk, bk), ("v", wv, bv)):
        wsb[nm] = p.sb(f"w{nm}_sb", [128, 8 * 1024], BF16)
        p.dma("g", wsb[nm].ap().rearrange("p (s n) -> p s n", s=8),
              w.rearrange("(s p) n -> p s n", p=128), writes=[(nm,)], tag=nm)
        bsb[nm] = p.sb(f"b{nm}_sb", [1, 1024], BF16)
        p.dma("g", bsb[nm][:], b, writes=[(nm + "b",)])
    cos_sb = p.sb("cos_sb", [128, 512], F32)
    sin_sb = p.sb("sin_sb", [128, 512], F32)
    qtf = cr.scratch("qtf", [128, 1024], F32)
    qtb = cr.scratch("qtb", [128, 1024], BF16)

    for t in range(NT):
        x_ap = cr.x_sb[:, t * D:(t + 1) * D]
        emit_ln_tile(cr, x_ap, [("x", t)], [(xnt[:], ("xnt",))], sidx=t % 2)
        emit_tok_to_fm(cr, xnt[:], [("xnt",)], t, xn_fm, ("xnfm",))

    for t in range(NT):
        p.dma("q", cos_sb[:], cosd.rearrange("(t p) n -> p t n", p=128)[:, t],
              writes=[("ctbl",)], tag="ctbl")
        p.dma("q", sin_sb[:], sind.rearrange("(t p) n -> p t n", p=128)[:, t],
              writes=[("ctbl",)], tag="stbl")
        for nm, outd, rope in (("q", q_out, True), ("k", k_out, True),
                               ("v", v_out, False)):
            def evac(dg, bank, bk2):
                p.op("v", lambda e, dg=dg, bank=bank: e.tensor_copy(
                    qtf[:, dg * 512:(dg + 1) * 512], bank[:]),
                    reads=[bk2], writes=[("qtf",)], tag="pj_evac")
            emit_proj_to_tok(cr, xn_fm, wsb[nm], (nm,), t, 8, 1024,
                             bias_sb=bsb[nm], bkey=(nm + "b",), out_cb=evac,
                             tag=f"{nm}mm")
            if rope:
                emit_rope_halfsplit(cr, qtf.ap(), ("qtf",), qtb.ap(), ("qtb",),
                                    cos_sb, sin_sb, ("ctbl",), -1, +1)
            else:
                p.op("v", lambda e: e.tensor_copy(qtb[:], qtf[:]),
                     reads=[("qtf",)], writes=[("qtb",)], tag="v_cast")
            if nm == "q":
                p.op("v", lambda e: e.tensor_scalar_mul(qtb[:], qtb[:], 0.125),
                     reads=[("qtb",)], writes=[("qtb",)], tag="q_scale")
            p.dma("q", outd[t * 128:(t + 1) * 128, :], qtb[:],
                  reads=[("qtb",)], tag="qkv_store")
    return finish_launch(p, cr, aps), aps


def build_l1_ffn_b(parity):
    p, cr, aps = begin_launch()
    load_masks(cr, aps["consts"])
    q_fm = p.din("q_fm", (8, 128, OWN), BF16)
    k_fm = p.din("k_fm", (8, 128, T), BF16)
    v_arr = p.din("v_arr", (16, 128, (T // 128) * 64), BF16)
    wo = p.din("l1f_Wo", (1024, 1024))
    bo = p.din("l1f_bo", (1, 1024))
    g1 = p.din("l1f_g1", (1, 1024))
    be1 = p.din("l1f_be1", (1, 1024))
    x1_out = p.dout("x1_out", (OWN, 1024))

    attn_fm = p.sb("attn_fm", [128, 8 * OWN], BF16)
    wo_sb = p.sb("wo_sb", [128, 8 * 1024], BF16)
    bo_sb = p.sb("bo_sb", [1, 1024], BF16)
    g1b = p.sb("g1b", [128, 1024], F32)
    be1b = p.sb("be1b", [128, 1024], F32)
    yt = cr.scratch("yt", [128, 1024], F32)
    xnt_f = cr.scratch("xnt_f", [128, 1024], F32)
    p.dma("g", wo_sb.ap().rearrange("p (s n) -> p s n", s=8),
          wo.rearrange("(s p) n -> p s n", p=128), writes=[("wo",)])
    p.dma("g", bo_sb[:], bo, writes=[("bo",)])

    g1r = p.sb("g1r", [1, 2048], F32)
    p.dma("q", g1r[0:1, 0:1024], g1, writes=[("g1r",)])
    p.dma("q", g1r[0:1, 1024:2048], be1, writes=[("g1r",)])
    for part, dst in ((0, g1b), (1, be1b)):
        for dg in range(2):
            bank, bk = cr.bank()
            p.op("t", lambda e, part=part, dg=dg, bank=bank: e.matmul(
                bank[:], cr.ones_row_f[0:1, 0:128],
                g1r[0:1, part * 1024 + dg * 512: part * 1024 + dg * 512 + 512],
                start=True, stop=True),
                reads=[("g1r",), ("ones_row_f",)], writes=[bk], tag="gbc")
            p.op("v", lambda e, dst=dst, dg=dg, bank=bank: e.tensor_copy(
                dst[:, dg * 512:(dg + 1) * 512], bank[:]),
                reads=[bk], writes=[("g1b",)], tag="gbc_cp")

    cfg = dict(
        nh=16, dq=64, dv=64, causal=False, parity=parity,
        q_parts=[(lambda h: q_fm[h // 2, (h % 2) * 64:(h % 2) * 64 + 64, :],
                  0, 64)],
        k_parts=[(lambda h: k_fm[h // 2, (h % 2) * 64:(h % 2) * 64 + 64, :],
                  0, 64)],
        v_arr=v_arr, attn_fm=attn_fm)
    emit_attention(cr, cfg)

    for t in range(NT):
        for dg in range(2):
            bank, bk = cr.bank()
            for s_ in range(8):
                p.op("t", lambda e, s_=s_, dg=dg, bank=bank, t=t: e.matmul(
                    bank[:],
                    attn_fm.ap().rearrange("p (s n) -> p s n", n=OWN)[
                        :, s_, t * 128:(t + 1) * 128],
                    wo_sb[:, s_ * 1024 + dg * 512: s_ * 1024 + dg * 512 + 512],
                    start=(s_ == 0), stop=False),
                    reads=[("attnfm",), ("wo",)], writes=[bk], tag="wo_mm")
            p.op("t", lambda e, dg=dg, bank=bank: e.matmul(
                bank[:], cr.ones_row[0:1, 0:128],
                bo_sb[0:1, dg * 512:(dg + 1) * 512], start=False, stop=True),
                reads=[("bo",), ("ones_row",)], writes=[bk], tag="wo_bias")
            p.op("v", lambda e, dg=dg, bank=bank: e.tensor_copy(
                yt[:, dg * 512:(dg + 1) * 512], bank[:]),
                reads=[bk], writes=[("yt",)], tag="wo_evac")
        x_ap = cr.x_sb[:, t * D:(t + 1) * D]
        emit_ln_tile(cr, x_ap, [("x", t)], [(xnt_f[:], ("xnt_f",))],
                     sidx=t % 2)
        p.op("v", lambda e: e.tensor_add(yt[:], yt[:], xnt_f[:]),
             reads=[("yt",), ("xnt_f",)], writes=[("yt",)], tag="x1_pre")
        emit_ln_tile(cr, yt.ap(), [("yt",)], [(xnt_f[:], ("xnt_f",))],
                     sidx=2)
        p.op("v", lambda e: e.tensor_mul(xnt_f[:], xnt_f[:], g1b[:]),
             reads=[("xnt_f",), ("g1b",)], writes=[("xnt_f",)], tag="x1_g")
        p.op("v", lambda e: e.tensor_add(xnt_f[:], xnt_f[:], be1b[:]),
             reads=[("xnt_f",), ("g1b",)], writes=[("xnt_f",)], tag="x1_b")
        p.dma("q", x1_out[t * 128:(t + 1) * 128, :], xnt_f[:],
              reads=[("xnt_f",)], tag="x1_store")
    return finish_launch(p, cr, aps), aps


def build_l1_ffn_c():
    p = Prog()
    x1_in = p.din("x1_in", (OWN, D))
    x_in = p.din("x_in", (OWN, D))
    consts = p.din("consts", (641, 1536))
    x_out = p.dout("x_out", (OWN, D))
    cr = Core(p)
    register_float_consts(cr, FLOAT_CONSTS)
    cr.load_consts(consts)
    x_sb = p.sb("x_sb", [128, NT * D], F32)
    cr.x_sb = x_sb
    p.dma("q", x_sb.ap().rearrange("p (t d) -> p t d", t=NT),
          x1_in.rearrange("(t p) d -> p t d", p=128),
          writes=[("x", t) for t in range(NT)], tag="load_x1")

    wg = p.din("l1f_Wg", (1024, 8))
    bg = p.din("l1f_bg", (1, 8))
    W1 = p.din("l1f_W1", (8, 1024, 4096))
    W2 = p.din("l1f_W2", (8, 4096, 1024))
    b1T = p.din("l1f_b1T", (8, 128, 32))
    b2 = p.din("l1f_b2", (8, 1024))
    g2 = p.din("l1f_g2", (1, 1024))
    be2 = p.din("l1f_be2", (1, 1024))
    NE = 8

    x1_fm = p.sb("x1_fm", [128, 8 * OWN], BF16)
    y_acc = p.sb("y_acc", [128, 8 * 1024], F32)
    h_fm = p.sb("h_fm", [128, 4 * 1024], BF16)
    w1c = p.sb("w1c", [128, 8 * 512], BF16)
    w2c = p.sb("w2c", [128, 4 * 1024], BF16)
    b1c = p.sb("b1c", [128, 4], F32)
    b2r = p.sb("b2r", [1, 1024], BF16)
    gateT_sb = p.sb("gateT_sb", [128, 8 * NE], F32)
    gb_sb = p.sb("gb_sb", [1, NE], F32)
    ge_sb = p.sb("ge_sb", [128, NT * NE], F32)
    g2b = p.sb("g2b", [128, 1024], F32)
    be2b = p.sb("be2b", [128, 1024], F32)
    xnt = cr.scratch("xnt", [128, D], BF16)
    xtile = cr.scratch("xtile", [128, D], F32)
    xorig = cr.scratch("xorig", [128, D], F32)
    yt = cr.scratch("yt", [128, D], F32)

    p.dma("q", gateT_sb.ap().rearrange("p (s e) -> p s e", e=NE),
          wg.rearrange("(s p) e -> p s e", p=128), writes=[("gateT",)])
    p.dma("q", gb_sb[:], bg, writes=[("gb",)])
    g2r = p.sb("g2r", [1, 2048], F32)
    p.dma("q", g2r[0:1, 0:1024], g2, writes=[("g2r",)])
    p.dma("q", g2r[0:1, 1024:2048], be2, writes=[("g2r",)])
    for part, dst in ((0, g2b), (1, be2b)):
        for dg in range(2):
            bank, bk = cr.bank()
            p.op("t", lambda e, part=part, dg=dg, bank=bank: e.matmul(
                bank[:], cr.ones_row_f[0:1, 0:128],
                g2r[0:1, part * 1024 + dg * 512: part * 1024 + dg * 512 + 512],
                start=True, stop=True),
                reads=[("g2r",), ("ones_row_f",)], writes=[bk], tag="gbc")
            p.op("v", lambda e, dst=dst, dg=dg, bank=bank: e.tensor_copy(
                dst[:, dg * 512:(dg + 1) * 512], bank[:]),
                reads=[bk], writes=[("g2b",)], tag="gbc_cp")

    for t in range(NT):
        x1t = x_sb[:, t * D:(t + 1) * D]
        p.op("v", lambda e, x1t=x1t: e.tensor_copy(xnt[:], x1t),
             reads=[("x", t)], writes=[("xnt",)], tag="x1_cast")
        emit_tok_to_fm(cr, xnt[:], [("xnt",)], t, x1_fm, ("xnfm",))
        emit_moe_gating(cr, NE, gateT_sb, gb_sb, t, x1t, ("x", t),
                        ge_sb, ("softmax_topk", 2), eps_norm=0.0)

    for tg2 in range(2):
        for e in range(NE):
            p.dma("g", b2r[:], b2[e:e + 1, :], writes=[("b2r",)], tag="b2load")
            for hc in range(8):
                p.dma("g", w1c.ap().rearrange("p (s n) -> p s n", s=8),
                      W1[e].rearrange("(s p) n -> p s n", p=128)
                      [:, :, hc * 512:(hc + 1) * 512],
                      writes=[("w1c",)], tag="w1load")
                p.dma("g", w2c.ap().rearrange("p (s n) -> p s n", s=4),
                      W2[e].rearrange("(s p) n -> p s n", p=128)
                      [:, hc * 4:(hc + 1) * 4, :],
                      writes=[("w2c",)], tag="w2load")
                p.dma("q", b1c[:], b1T[e, :, hc * 4:(hc + 1) * 4],
                      writes=[("b1c",)], tag="b1load")
                for hs in range(4):
                    for sg in range(2):
                        tok0 = tg2 * 1024 + sg * 512
                        bank, bk = cr.bank()
                        for s_ in range(8):
                            p.op("t", lambda e2, s_=s_, hs=hs, tok0=tok0,
                                 bank=bank: e2.matmul(
                                     bank[:],
                                     w1c[:, s_ * 512 + hs * 128:
                                         s_ * 512 + (hs + 1) * 128],
                                     x1_fm[:, s_ * OWN + tok0:
                                           s_ * OWN + tok0 + 512],
                                     start=(s_ == 0), stop=(s_ == 7)),
                                 reads=[("w1c",), ("xnfm",)], writes=[bk],
                                 tag="h_mm")
                        p.op("s", lambda e2, hs=hs, sg=sg, bank=bank:
                             e2.activation(
                                 h_fm[:, hs * 1024 + sg * 512:
                                      hs * 1024 + (sg + 1) * 512],
                                 bank[:], AF.Gelu, bias=b1c[:, hs:hs + 1]),
                             reads=[bk, ("b1c",)], writes=[("hfm",)],
                             tag="h_gelu")
                for tl in range(8):
                    t = tg2 * 8 + tl
                    for dg in range(2):
                        bank, bk = cr.bank()
                        for hs in range(4):
                            p.op("t", lambda e2, hs=hs, tl=tl, dg=dg,
                                 bank=bank, hc=hc: e2.matmul(
                                     bank[:],
                                     h_fm[:, hs * 1024 + tl * 128:
                                          hs * 1024 + tl * 128 + 128],
                                     w2c[:, hs * 1024 + dg * 512:
                                         hs * 1024 + dg * 512 + 512],
                                     start=(hs == 0),
                                     stop=(hs == 3 and hc != 0)),
                                 reads=[("hfm",), ("w2c",)], writes=[bk],
                                 tag="dn_mm")
                        if hc == 0:
                            p.op("t", lambda e2, dg=dg, bank=bank: e2.matmul(
                                bank[:], cr.ones_row[0:1, 0:128],
                                b2r[0:1, dg * 512:(dg + 1) * 512],
                                start=False, stop=True),
                                reads=[("b2r",), ("ones_row",)], writes=[bk],
                                tag="dn_b2")
                        ydst = y_acc[:, tl * D + dg * 512:
                                     tl * D + dg * 512 + 512]
                        if e == 0 and hc == 0:
                            p.op("v", lambda e2, ydst=ydst, bank=bank, t=t,
                                 e=e: e2.tensor_scalar_mul(
                                     ydst, bank[:],
                                     ge_sb[:, t * NE + e: t * NE + e + 1]),
                                 reads=[bk, ("ge", t)],
                                 writes=[("yacc", t, dg)], tag="dn_sc0")
                        else:
                            p.op("v", lambda e2, ydst=ydst, bank=bank, t=t,
                                 e=e: e2.scalar_tensor_tensor(
                                     ydst, bank[:],
                                     ge_sb[:, t * NE + e: t * NE + e + 1],
                                     ydst, ALU.mult, ALU.add),
                                 reads=[bk, ("ge", t), ("yacc", t, dg)],
                                 writes=[("yacc", t, dg)], tag="dn_scacc")
        for tl in range(8):
            t = tg2 * 8 + tl
            x1t = x_sb[:, t * D:(t + 1) * D]
            yts = y_acc[:, tl * D:(tl + 1) * D]
            p.op("v", lambda e, x1t=x1t, yts=yts: e.tensor_add(yts, yts, x1t),
                 reads=[("x", t), ("yacc", t, 0), ("yacc", t, 1)],
                 writes=[("yacc", t, 0), ("yacc", t, 1)], tag="x2_pre")
            emit_ln_tile(cr, yts, [("yacc", t, 0), ("yacc", t, 1)],
                         [(yt[:], ("ytl",))], sidx=2)
            p.op("v", lambda e: e.tensor_mul(yt[:], yt[:], g2b[:]),
                 reads=[("ytl",), ("g2b",)], writes=[("ytl",)], tag="x2_g")
            p.op("v", lambda e: e.tensor_add(yt[:], yt[:], be2b[:]),
                 reads=[("ytl",), ("g2b",)], writes=[("ytl",)], tag="x2_b")
            emit_ln_tile(cr, yt.ap(), [("ytl",)], [(xtile[:], ("xtl",))],
                         sidx=3)
            p.dma("q", xorig[:], x_in[t * 128:(t + 1) * 128, :],
                  writes=[("xorig",)], tag="x_reload")
            p.op("v", lambda e: e.tensor_add(xtile[:], xtile[:], xorig[:]),
                 reads=[("xtl",), ("xorig",)], writes=[("xtl",)], tag="x_add")
            p.dma("q", x_out[t * 128:(t + 1) * 128, :], xtile[:],
                  reads=[("xtl",)], tag="x_store")
    st_op = p.all_ops[-1]
    st_op.needs_inc = True
    nop = p.op("q", lambda e: e.nop(), tag="final_nop")
    nop.deps.append(st_op)
    return p.build(), None


def run_l1_ffn(x_shards, inp, run):
    consts = np_consts()
    extra = l1_ffn_host_inputs(inp)
    nc_a, _ = build_l1_ffn_a()
    in_maps = []
    for c in range(NCORES):
        m = dict(x_in=x_shards[c], consts=consts,
                 l1f_cos=extra["l1f_cos"](c), l1f_sin=extra["l1f_sin"](c))
        for k in ("l1f_Wq", "l1f_Wk", "l1f_Wv", "l1f_bq", "l1f_bk", "l1f_bv"):
            m[k] = extra[k]
        in_maps.append(m)
    resA = run(nc_a, in_maps)
    k_full = {}
    v_full = {}
    for b_ in range(B):
        kf = np.zeros((T, 1024), dtype=BF)
        vf = np.zeros((T, 1024), dtype=BF)
        for par in range(2):
            c = 2 * b_ + par
            kf[own_index(par)] = resA[c]["k_out"]
            vf[own_index(par)] = resA[c]["v_out"]
        k_full[b_] = host_fm_rearrange(kf)
        v_full[b_] = host_v_arrange(vf)
    nc_b = {}
    for par in range(2):
        nc_b[par], _ = build_l1_ffn_b(par)
    x1_sh = [None] * NCORES
    for par in range(2):
        cores = [2 * b_ + par for b_ in range(B)]
        in_maps = []
        for b_ in range(B):
            c = 2 * b_ + par
            in_maps.append(dict(
                x_in=x_shards[c], consts=consts,
                q_fm=host_fm_rearrange(resA[c]["q_out"]),
                k_fm=k_full[b_], v_arr=v_full[b_],
                l1f_Wo=extra["l1f_Wo"], l1f_bo=extra["l1f_bo"],
                l1f_g1=extra["l1f_g1"], l1f_be1=extra["l1f_be1"]))
        resB = run(nc_b[par], in_maps, core_ids=cores)
        for i, c in enumerate(cores):
            x1_sh[c] = resB[i]["x1_out"]
    nc_c, _ = build_l1_ffn_c()
    in_maps = []
    for c in range(NCORES):
        m = dict(x1_in=x1_sh[c], x_in=x_shards[c], consts=consts)
        for k in ("l1f_Wg", "l1f_bg", "l1f_W1", "l1f_W2", "l1f_b1T", "l1f_b2",
                  "l1f_g2", "l1f_be2"):
            m[k] = extra[k]
        in_maps.append(m)
    resC = run(nc_c, in_maps)
    return [resC[c]["x_out"] for c in range(NCORES)]



def l3_attn_host_inputs(inp):
    f32 = np.float32
    Wqkv = inp["L3_attn_W_qkv"].astype(f32)
    d = {
        "l3a_wqT": np.ascontiguousarray(Wqkv[0:1024].T),
        "l3a_wkT": np.ascontiguousarray(Wqkv[1024:2048].T),
        "l3a_wvT": np.ascontiguousarray(Wqkv[2048:3072].T),
        "l3a_woT": np.ascontiguousarray(inp["L3_attn_W_out"].T).astype(f32),
        "l3a_hyp": inp["L3_attn_hyp_k"].reshape(1, 16).astype(f32),
    }
    jj = np.arange(0, 64, 2, dtype=np.float64)
    invf = 1.0 / 10000.0 ** (jj / 64)
    tpos = np.arange(T, dtype=np.float64)
    fr = tpos[:, None] * invf[None, :]
    cos = np.cos(fr)
    sin = np.sin(fr)
    for c in range(NCORES):
        pos = own_index(c % 2)
        d.setdefault("_c", {})[c] = np.ascontiguousarray(
            np.tile(cos[pos], (1, 16)).astype(f32))
        d.setdefault("_s", {})[c] = np.ascontiguousarray(
            np.tile(sin[pos], (1, 16)).astype(f32))
    d["l3a_cos"] = (lambda m: (lambda c: m[c]))(d.pop("_c"))
    d["l3a_sin"] = (lambda m: (lambda c: m[c]))(d.pop("_s"))
    return d


def build_l3_attn_a():
    p, cr, aps = begin_launch()
    wq = p.din("l3a_wqT", (1024, 1024))
    wk = p.din("l3a_wkT", (1024, 1024))
    wv = p.din("l3a_wvT", (1024, 1024))
    hyp = p.din("l3a_hyp", (1, 16))
    cosd = p.din("l3a_cos", (OWN, 512))
    sind = p.din("l3a_sin", (OWN, 512))
    q_out = p.dout("q_out", (OWN, 1024), BF16)
    k_out = p.dout("k_out", (OWN, 1024), BF16)
    v_out = p.dout("v_out", (OWN, 1024), BF16)
    q0_out = p.dout("q0_out", (OWN, 16))
    k0_out = p.dout("k0_out", (OWN, 16))

    xn_fm = p.sb("xn_fm", [128, 8 * OWN], BF16)
    xnt = cr.scratch("xnt", [128, D], BF16)
    wsb = {}
    for nm, w in (("q", wq), ("k", wk), ("v", wv)):
        wsb[nm] = p.sb(f"w{nm}_sb", [128, 8 * 1024], BF16)
        p.dma("g", wsb[nm].ap().rearrange("p (s n) -> p s n", s=8),
              w.rearrange("(s p) n -> p s n", p=128), writes=[(nm,)], tag=nm)
    cos_sb = p.sb("cos_sb", [128, 512], F32)
    sin_sb = p.sb("sin_sb", [128, 512], F32)
    kcb = p.sb("kcb", [128, 16], F32)
    qtf = cr.scratch("qtf", [128, 1024], F32)
    qtf2 = cr.scratch("qtf2", [128, 1024], F32)
    qtb = cr.scratch("qtb", [128, 1024], BF16)
    sqs = cr.scratch("sqs", [128, 1024], F32)
    n0 = cr.scratch("n0", [128, 16], F32)

    hyp_sb = p.sb("hyp_sb", [1, 16], F32)
    p.dma("q", hyp_sb[:], hyp, writes=[("hyp",)])
    bank, bk = cr.bank()
    p.op("t", lambda e: e.matmul(bank[:, 0:16], cr.ones_row_f[0:1, 0:128],
                                 hyp_sb[0:1, :], start=True, stop=True),
         reads=[("hyp",), ("ones_row_f",)], writes=[bk], tag="hyp_bc")
    p.op("v", lambda e: e.tensor_copy(kcb[:], bank[:, 0:16]),
         reads=[bk], writes=[("kcb",)], tag="hyp_cp")

    for t in range(NT):
        x_ap = cr.x_sb[:, t * D:(t + 1) * D]
        emit_ln_tile(cr, x_ap, [("x", t)], [(xnt[:], ("xnt",))], sidx=t % 2)
        emit_tok_to_fm(cr, xnt[:], [("xnt",)], t, xn_fm, ("xnfm",))

    for t in range(NT):
        p.dma("q", cos_sb[:], cosd.rearrange("(t p) n -> p t n", p=128)[:, t],
              writes=[("ctbl",)], tag="ctbl")
        p.dma("q", sin_sb[:], sind.rearrange("(t p) n -> p t n", p=128)[:, t],
              writes=[("ctbl",)], tag="stbl")
        for nm, outd, n0d, rope in (("q", q_out, q0_out, True),
                                    ("k", k_out, k0_out, True),
                                    ("v", v_out, None, False)):
            def evac(dg, bank2, bk2):
                p.op("v", lambda e, dg=dg, bank2=bank2: e.tensor_copy(
                    qtf[:, dg * 512:(dg + 1) * 512], bank2[:]),
                    reads=[bk2], writes=[("qtf",)], tag="pj_evac")
            emit_proj_to_tok(cr, xn_fm, wsb[nm], (nm,), t, 8, 1024,
                             out_cb=evac, tag=f"{nm}mm")
            if rope:
                emit_rope_halfsplit(cr, qtf.ap(), ("qtf",), qtf2.ap(),
                                    ("qtf2",), cos_sb, sin_sb, ("ctbl",),
                                    +1, -1)
                p.op("v", lambda e: e.tensor_copy(qtb[:], qtf2[:]),
                     reads=[("qtf2",)], writes=[("qtb",)], tag="rope_cast")
                p.op("s", lambda e: e.activation(sqs[:], qtb[:], AF.Square),
                     reads=[("qtb",)], writes=[("sqs",)], tag="n0_sq")
                p.op("v", lambda e: e.reduce_sum(
                    n0[:], sqs.ap().rearrange("p (h j) -> p h j", j=64),
                    axis=AX.X),
                    reads=[("sqs",)], writes=[("n0",)], tag="n0_red")
                p.op("v", lambda e: e.tensor_add(n0[:], n0[:], kcb[:]),
                     reads=[("n0",), ("kcb",)], writes=[("n0",)], tag="n0_kc")
                p.op("s", lambda e: e.activation(n0[:], n0[:], AF.Sqrt),
                     reads=[("n0",)], writes=[("n0",)], tag="n0_sqrt")
                p.dma("q", n0d[t * 128:(t + 1) * 128, :], n0[:],
                      reads=[("n0",)], tag="n0_store")
            else:
                p.op("v", lambda e: e.tensor_copy(qtb[:], qtf[:]),
                     reads=[("qtf",)], writes=[("qtb",)], tag="v_cast")
            p.dma("q", outd[t * 128:(t + 1) * 128, :], qtb[:],
                  reads=[("qtb",)], tag="qkv_store")
    return finish_launch(p, cr, aps), aps


def build_l3_attn_b(parity):
    p, cr, aps = begin_launch()
    load_masks(cr, aps["consts"])
    q_fm = p.din("q_fm", (8, 128, OWN), BF16)
    k_fm = p.din("k_fm", (8, 128, T), BF16)
    v_arr = p.din("v_arr", (16, 128, (T // 128) * 64), BF16)
    q0T = p.din("q0T", (16, OWN))
    k0T = p.din("k0T", (16, T))
    hyp = p.din("l3a_hyp", (1, 16))
    wo = p.din("l3a_woT", (1024, 1024))

    attn_fm = p.sb("attn_fm", [128, 8 * OWN], BF16)
    wo_sb = p.sb("wo_sb", [128, 8 * 1024], BF16)
    yt = cr.scratch("yt", [128, 1024], F32)
    qa = [p.sb("qaug0", [128, OWN], BF16)]
    ka = [p.sb("kaug0", [128, T], BF16)]
    vh = [p.sb("vh0", [128, (T // 128) * 64], BF16)]
    v1row = p.sb("v1row", [128, T], F32)
    wrow = p.sb("wrow", [128, T], F32)
    prow = p.sb("prow", [128, T], BF16)
    ptc = [p.sb(f"ptc{i}", [128, 128], BF16) for i in range(2)]
    nrk = p.sb("nrk", [128, 16 * 3], F32)
    mst = p.sb("mst", [128, 8], F32)
    aug1 = p.sb("aug1", [1, T], BF16)

    p.dma("g", wo_sb.ap().rearrange("p (s n) -> p s n", s=8),
          wo.rearrange("(s p) n -> p s n", p=128), writes=[("wo",)])
    hyp_sb = p.sb("hyp_sb", [1, 16], F32)
    p.dma("q", hyp_sb[:], hyp, writes=[("hyp",)])
    bank, bk = cr.bank()
    p.op("t", lambda e: e.matmul(bank[:, 0:16], cr.ones_row_f[0:1, 0:128],
                                 hyp_sb[0:1, :], start=True, stop=True),
         reads=[("hyp",), ("ones_row_f",)], writes=[bk], tag="hyp_bc")
    p.op("v", lambda e: e.reciprocal(nrk[:, 16:32], bank[:, 0:16]),
         reads=[bk], writes=[("nrk",)], tag="hyp_rcp")
    p.op("v", lambda e: e.tensor_scalar_mul(nrk[:, 0:16], nrk[:, 16:32], -1.0),
         reads=[("nrk",)], writes=[("nrk",)], tag="hyp_neg")
    p.op("s", lambda e: e.activation(nrk[:, 32:48], bank[:, 0:16], AF.Sqrt),
         reads=[bk, ("nrk",)], writes=[("nrk",)], tag="hyp_sqrt")

    NKT = T // 128

    def q_abs(qt):
        return tile_pos(parity, qt)

    for h in range(16):
        hb = 0
        qkey, kkey, vkey = ("qa", 0), ("ka", 0), ("vh", 0)
        p.dma("q", qa[hb][0:64, :], q_fm[h // 2, (h % 2) * 64:(h % 2) * 64 + 64, :],
              writes=[qkey], tag="qload")
        p.dma("q", ka[hb][0:64, :], k_fm[h // 2, (h % 2) * 64:(h % 2) * 64 + 64, :],
              writes=[kkey], tag="kload")
        p.dma("q", wrow[0:1, 0:OWN], q0T[h:h + 1, :], writes=[("wrow",)],
              tag="q0load")
        p.op("v", lambda e, hb=hb: e.tensor_copy(qa[hb][64:65, :],
                                                 wrow[0:1, 0:OWN]),
             reads=[("wrow",)], writes=[qkey], tag="q0hi")
        p.op("v", lambda e, hb=hb: e.tensor_copy(v1row[0:1, 0:OWN],
                                                 qa[hb][64:65, :]),
             reads=[qkey], writes=[("v1row",)], tag="q0hif")
        p.op("v", lambda e: e.tensor_sub(wrow[0:1, 0:OWN], wrow[0:1, 0:OWN],
                                         v1row[0:1, 0:OWN]),
             reads=[("wrow",), ("v1row",)], writes=[("wrow",)], tag="q0lo")
        p.op("v", lambda e: e.tensor_copy(aug1[0:1, 0:OWN],
                                          wrow[0:1, 0:OWN]),
             reads=[("wrow",)], writes=[("aug1",)], tag="q0locast")
        p.dma("g", qa[hb][65:66, :], aug1[0:1, 0:OWN], reads=[("aug1",)],
              writes=[qkey], tag="q0locp")
        p.dma("g", qa[hb][66:67, :], qa[hb][64:65, :], reads=[qkey],
              writes=[qkey], tag="q0hicp")
        p.dma("q", wrow[0:1, :], k0T[h:h + 1, :], writes=[("wrow",)],
              tag="k0load")
        p.op("v", lambda e, hb=hb: e.tensor_scalar_mul(
            ka[hb][64:65, :], wrow[0:1, :], -1.0),
            reads=[("wrow",)], writes=[kkey], tag="k0hi")
        p.dma("g", ka[hb][65:66, :], ka[hb][64:65, :], reads=[kkey],
              writes=[kkey], tag="k0hicp")
        p.op("v", lambda e, hb=hb: e.tensor_copy(v1row[0:1, :],
                                                 ka[hb][64:65, :]),
             reads=[kkey], writes=[("v1row",)], tag="k0hif")
        p.op("v", lambda e: e.tensor_scalar_mul(wrow[0:1, :], wrow[0:1, :],
                                                -1.0),
             reads=[("wrow",)], writes=[("wrow",)], tag="k0neg")
        p.op("v", lambda e: e.tensor_sub(wrow[0:1, :], wrow[0:1, :],
                                         v1row[0:1, :]),
             reads=[("wrow",), ("v1row",)], writes=[("wrow",)], tag="k0lo")
        p.op("v", lambda e: e.tensor_copy(aug1[0:1, :], wrow[0:1, :]),
             reads=[("wrow",)], writes=[("aug1",)], tag="k0locast")
        p.dma("g", ka[hb][66:67, :], aug1[0:1, :], reads=[("aug1",)],
              writes=[kkey], tag="k0locp")
        p.dma("q", vh[hb][:], v_arr[h], writes=[vkey], tag="vload")

        for qt in range(NT):
            nks = ceil_div(q_abs(qt) + 128, 512)
            nkt = ceil_div(q_abs(qt) + 128, 128)
            for ks in range(nks):
                bankS, bkS = cr.pbank(4 + ks % 2)
                p.op("t", lambda e, bankS=bankS, ks=ks, qt=qt, hb=hb: e.matmul(
                    bankS[:], qa[hb][0:67, qt * 128:(qt + 1) * 128],
                    ka[hb][0:67, ks * 512:(ks + 1) * 512],
                    start=True, stop=True),
                    reads=[qkey, kkey], writes=[bkS], tag="l3_mm")
                sl = v1row[:, ks * 512:(ks + 1) * 512]
                p.op("v", lambda e, bankS=bankS, sl=sl, h=h: e.tensor_scalar(
                    sl, bankS[:], nrk[:, h:h + 1], 1.0 + 1e-6,
                    ALU.mult, ALU.max),
                    reads=[bkS, ("nrk",)], writes=[("v1row",)], tag="l3_arg")
            ex = nks * 512
            vr = v1row[:, 0:ex]
            wr = wrow[:, 0:ex]
            p.op("s", lambda e, wr=wr, vr=vr: e.activation(wr, vr, AF.Square),
                 reads=[("v1row",)], writes=[("wrow",)], tag="l3_sq")
            p.op("s", lambda e, wr=wr: e.activation(wr, wr, AF.Sqrt,
                                                    bias=-1.0),
                 reads=[("wrow",)], writes=[("wrow",)], tag="l3_rt")
            p.op("v", lambda e, wr=wr, vr=vr: e.tensor_add(vr, vr, wr),
                 reads=[("wrow",), ("v1row",)], writes=[("v1row",)],
                 tag="l3_t3")
            p.op("s", lambda e, vr=vr: e.activation(vr, vr, AF.Ln),
                 reads=[("v1row",)], writes=[("v1row",)], tag="l3_lnt3")
            p.op("s", lambda e, vr=vr, h=h: e.activation(
                vr, vr, AF.Ln, scale=nrk[:, 32 + h:33 + h], bias=1e-6),
                reads=[("v1row",), ("nrk",)], writes=[("v1row",)],
                tag="l3_lnv1")
            ksx = q_abs(qt) // 512
            delta = q_abs(qt) - ksx * 512
            p.op("v", lambda e, ksx=ksx, delta=delta: e.tensor_sub(
                v1row[:, ksx * 512:(ksx + 1) * 512],
                v1row[:, ksx * 512:(ksx + 1) * 512],
                cr.maskG[:, 384 - delta:384 - delta + 512]),
                reads=[("v1row",), ("maskG",)], writes=[("v1row",)],
                tag="l3_mask")
            p.op("v", lambda e, vr=vr: e.tensor_reduce(
                mst[:, 0:1], vr, axis=AX.X, op=ALU.min),
                reads=[("v1row",)], writes=[("mst",)], tag="l3_min")
            p.op("s", lambda e: e.activation(mst[:, 1:2], mst[:, 0:1],
                                             AF.Exp, scale=-1.0),
                 reads=[("mst",)], writes=[("mst",)], tag="l3_wmax")
            p.op("v", lambda e: e.tensor_scalar_mul(mst[:, 2:3], mst[:, 1:2],
                                                    -1.0),
                 reads=[("mst",)], writes=[("mst",)], tag="l3_negwm")
            p.op("s", lambda e, wr=wr, vr=vr: e.activation(
                wr, vr, AF.Exp, scale=-1.0),
                reads=[("v1row",)], writes=[("wrow",)], tag="l3_w")
            p.op("v", lambda e, ksx=ksx, delta=delta: e.tensor_add(
                wrow[:, ksx * 512:(ksx + 1) * 512],
                wrow[:, ksx * 512:(ksx + 1) * 512],
                cr.maskG[:, 384 - delta:384 - delta + 512]),
                reads=[("wrow",), ("maskG",)], writes=[("wrow",)],
                tag="l3_wmask")
            pr = prow[:, 0:ex]
            p.op("s", lambda e, pr=pr, wr=wr: e.activation(
                pr, wr, AF.Exp, bias=mst[:, 2:3], accum_out=mst[:, 3:4]),
                reads=[("wrow",), ("mst",)], writes=[("prow",), ("mst",)],
                tag="l3_p")
            p.op("v", lambda e: e.reciprocal(mst[:, 4:5], mst[:, 3:4]),
                 reads=[("mst",)], writes=[("mst",)], tag="l3_rinv")
            p.op("v", lambda e, pr=pr: e.tensor_scalar_mul(pr, pr,
                                                           mst[:, 4:5]),
                 reads=[("prow",), ("mst",)], writes=[("prow",)],
                 tag="l3_pnorm")
            avb, avk = cr.pbank(qt % 4)
            for kt in range(nkt):
                tb, tbk = cr.tbank()
                p.op("t", lambda e, tb=tb, kt=kt: e.matmul(
                    tb[:, 0:128], prow[:, kt * 128:(kt + 1) * 128],
                    cr.identb[:], start=True, stop=True, is_transpose=True),
                    reads=[("prow",), ("identb",)], writes=[tbk], tag="l3_tr")
                eng = "v" if kt % 2 == 0 else "s"
                pc = ptc[kt % 2]
                if eng == "v":
                    p.op("v", lambda e, pc=pc, tb=tb: e.tensor_copy(
                        pc[:], tb[:, 0:128]),
                        reads=[tbk], writes=[("ptc", kt % 2)], tag="l3_ptcp")
                else:
                    p.op("s", lambda e, pc=pc, tb=tb: e.copy(pc[:], tb[:, 0:128]),
                         reads=[tbk], writes=[("ptc", kt % 2)], tag="l3_ptcp")
                p.op("t", lambda e, avb=avb, pc=pc, kt=kt, hb=hb,
                     nkt=nkt: e.matmul(
                         avb[0:64, 0:128], vh[hb][:, kt * 64:(kt + 1) * 64],
                         pc[:], start=(kt == 0), stop=(kt == nkt - 1)),
                     reads=[("ptc", kt % 2), vkey], writes=[avk], tag="l3_av")
            slab = (h * 64) // 128
            r0 = (h * 64) % 128
            dst = attn_fm.ap().rearrange("p (s n) -> p s n", n=OWN)[
                r0:r0 + 64, slab, qt * 128:(qt + 1) * 128]
            p.op("v", lambda e, avb=avb, dst=dst: e.tensor_copy(
                dst, avb[0:64, 0:128]),
                reads=[avk], writes=[("attnfm",)], tag="l3_avev")

    for t in range(NT):
        for dg in range(2):
            bank2, bk2 = cr.bank()
            for s_ in range(8):
                p.op("t", lambda e, s_=s_, dg=dg, bank2=bank2, t=t: e.matmul(
                    bank2[:],
                    attn_fm.ap().rearrange("p (s n) -> p s n", n=OWN)[
                        :, s_, t * 128:(t + 1) * 128],
                    wo_sb[:, s_ * 1024 + dg * 512: s_ * 1024 + dg * 512 + 512],
                    start=(s_ == 0), stop=(s_ == 7)),
                    reads=[("attnfm",), ("wo",)], writes=[bk2], tag="wo_mm")
            p.op("v", lambda e, dg=dg, bank2=bank2: e.tensor_copy(
                yt[:, dg * 512:(dg + 1) * 512], bank2[:]),
                reads=[bk2], writes=[("yt",)], tag="wo_evac")
        emit_residual_ln_add(cr, yt.ap(), [("yt",)], t, sidx=2)
    return finish_launch(p, cr, aps), aps


def run_l3_attn(x_shards, inp, run):
    consts = np_consts()
    extra = l3_attn_host_inputs(inp)
    nc_a, _ = build_l3_attn_a()
    in_maps = []
    for c in range(NCORES):
        m = dict(x_in=x_shards[c], consts=consts, l3a_hyp=extra["l3a_hyp"],
                 l3a_wqT=extra["l3a_wqT"], l3a_wkT=extra["l3a_wkT"],
                 l3a_wvT=extra["l3a_wvT"],
                 l3a_cos=extra["l3a_cos"](c), l3a_sin=extra["l3a_sin"](c))
        in_maps.append(m)
    resA = run(nc_a, in_maps)
    k_full = {}
    v_full = {}
    k0_full = {}
    for b_ in range(B):
        kf = np.zeros((T, 1024), dtype=BF)
        vf = np.zeros((T, 1024), dtype=BF)
        k0 = np.zeros((T, 16), dtype=np.float32)
        for par in range(2):
            c = 2 * b_ + par
            kf[own_index(par)] = resA[c]["k_out"]
            vf[own_index(par)] = resA[c]["v_out"]
            k0[own_index(par)] = resA[c]["k0_out"]
        k_full[b_] = host_fm_rearrange(kf)
        v_full[b_] = host_v_arrange(vf)
        k0_full[b_] = np.ascontiguousarray(k0.T)
    nc_b = {}
    for par in range(2):
        nc_b[par], _ = build_l3_attn_b(par)
    out_shards = [None] * NCORES
    for par in range(2):
        cores = [2 * b_ + par for b_ in range(B)]
        in_maps = []
        for b_ in range(B):
            c = 2 * b_ + par
            in_maps.append(dict(
                x_in=x_shards[c], consts=consts,
                q_fm=host_fm_rearrange(resA[c]["q_out"]),
                k_fm=k_full[b_], v_arr=v_full[b_],
                q0T=np.ascontiguousarray(resA[c]["q0_out"].T),
                k0T=k0_full[b_], l3a_hyp=extra["l3a_hyp"],
                l3a_woT=extra["l3a_woT"]))
        resB = run(nc_b[par], in_maps, core_ids=cores)
        for i, c in enumerate(cores):
            out_shards[c] = resB[i]["x_out"]
    return out_shards



def l2_attn_host_inputs(inp):
    f32 = np.float32
    d = {
        "l2a_inpT": np.ascontiguousarray(inp["L2_attn_in_proj_w"].T).astype(f32),
        "l2a_xpwT": np.ascontiguousarray(inp["L2_attn_x_proj_w"].T).astype(f32),
        "l2a_dtwT": np.ascontiguousarray(inp["L2_attn_dt_proj_w"].T).astype(f32),
        "l2a_opT": np.ascontiguousarray(inp["L2_attn_out_proj_w"].T).astype(f32),
        "l2a_convw": np.ascontiguousarray(inp["L2_attn_conv_w"]).astype(f32),
        "l2a_convbT": np.ascontiguousarray(
            inp["L2_attn_conv_b"].reshape(16, 128).T).astype(f32),
        "l2a_dtbT": np.ascontiguousarray(
            inp["L2_attn_dt_proj_b"].reshape(16, 128).T).astype(f32),
        "l2a_alog": np.ascontiguousarray(inp["L2_attn_A_log"]).astype(f32),
        "l2a_DT": np.ascontiguousarray(
            inp["L2_attn_D"].reshape(16, 128).T).astype(f32),
        "l2a_convwT": np.ascontiguousarray(
            inp["L2_attn_conv_w"].reshape(16, 128, 4).transpose(1, 0, 2)
            .reshape(128, 64)).astype(f32),
    }
    return d


def build_l2_attn_b(parity):
    p = Prog()
    consts = p.din("consts", (641, 1536))
    xn_fm_d = p.din("xn_fm", (8, 128, T), BF16)
    inpT = p.din("l2a_inpT", (1024, 4096))
    xpwT = p.din("l2a_xpwT", (2048, 96))
    dtwT = p.din("l2a_dtwT", (64, 2048))
    convwT = p.din("l2a_convwT", (128, 64))
    convbT = p.din("l2a_convbT", (128, 16))
    dtbT = p.din("l2a_dtbT", (128, 16))
    alog = p.din("l2a_alog", (2048, 16))
    DT = p.din("l2a_DT", (128, 16))
    y_out = p.dout("y_out", (8, 128, T), BF16)
    cr = Core(p)
    register_float_consts(cr, FLOAT_CONSTS)
    cr.load_consts(consts)

    sl0 = parity * 8

    xs_scr = p.dtmp("xs_scr", (16, 128, T), BF16)
    zg_scr = p.dtmp("zg_scr", (8, 128, T), BF16)
    dt_scr = p.dtmp("dt_scr", (8, 128, T))
    dtxs_scr = p.dtmp("dtxs_scr", (8, 128, T))
    bc_scr = p.dtmp("bc_scr", (2, 16, 128, T), BF16)

    xn_sb = p.sb("xn_sb", [128, 8 * 512], BF16)
    wcol = p.sb("wcol", [128, 2 * 128], BF16)
    raw = p.sb("raw", [128, T + 4], F32)
    xsb = p.sb("xsb", [128, T], BF16)
    xdbl = p.sb("xdbl", [128, 4096], F32)
    scr1 = p.sb("scr1", [128, T], F32)
    scr2 = p.sb("scr2", [128, T], F32)
    scr3 = p.sb("scr3", [128, T], F32)
    ybuf = p.sb("ybuf", [128, T], F32)
    bcb = p.sb("bcb", [128, 2 * T], BF16)
    cwcol = p.sb("cwcol", [128, 64], F32)
    cbcol = p.sb("cbcol", [128, 16], F32)
    dtbcol = p.sb("dtbcol", [128, 16], F32)
    acol = p.sb("acol", [128, 16], F32)
    dcol = p.sb("dcol", [128, 16], F32)
    xpw_sb = p.sb("xpw_sb", [128, 16 * 96], BF16)
    dtw_sb = p.sb("dtw_sb", [64, 2048], BF16)

    p.dma("g", cwcol[:], convwT, writes=[("cw",)])
    p.dma("q", cbcol[:], convbT, writes=[("cb",)])
    p.dma("q", dtbcol[:], dtbT, writes=[("dtb",)])
    p.dma("q", dcol[:], DT, writes=[("dc",)])
    p.dma("g", xpw_sb.ap().rearrange("p (s n) -> p s n", s=16),
          xpwT.rearrange("(s p) n -> p s n", p=128), writes=[("xpw",)])
    p.dma("g", dtw_sb[:], dtwT, writes=[("dtw",)])

    p.op("v", lambda e: e.memset(xdbl[0:96, :], 0.0), writes=[("xdbl",)])
    p.op("g", lambda e: e.memset(raw[:, 0:4], 0.0), writes=[("rawpad",)])

    for sl in range(24):
        is_z = sl >= 16
        out_row = sl * 128 if sl < 16 else 2048 + sl0 * 128 + (sl - 16) * 128
        for tg in range(T // 512):
            p.dma("g", xn_sb.ap().rearrange("p (s n) -> p s n", s=8),
                  xn_fm_d[:, :, tg * 512:(tg + 1) * 512]
                  .rearrange("s p n -> p s n"), writes=[("xn",)], tag="xnld")
            bank, bk = cr.bank()
            for s_ in range(8):
                bi = s_ % 2
                p.dma("g", wcol.ap().rearrange("p (b n) -> p b n", b=2)[:, bi],
                      inpT.rearrange("(s p) n -> p s n", p=128)
                      [:, s_, out_row:out_row + 128],
                      writes=[("wcol", bi)], tag="wcolld")
                p.op("t", lambda e, s_=s_, bi=bi, bank=bank, tg=tg: e.matmul(
                    bank[:],
                    wcol.ap().rearrange("p (b n) -> p b n", b=2)[:, bi],
                    xn_sb.ap().rearrange("p (s n) -> p s n", s=8)[:, s_],
                    start=(s_ == 0), stop=(s_ == 7)),
                    reads=[("wcol", bi), ("xn",)], writes=[bk], tag="ip_mm")
            if is_z:
                p.op("s", lambda e, bank=bank, tg=tg: e.activation(
                    xsb[:, tg * 512:(tg + 1) * 512], bank[:], AF.Silu),
                    reads=[bk], writes=[("xsb",)], tag="z_silu")
            else:
                p.op("v", lambda e, bank=bank, tg=tg: e.tensor_copy(
                    raw[:, 4 + tg * 512: 4 + (tg + 1) * 512], bank[:]),
                    reads=[bk, ("rawpad",)], writes=[("raw",)], tag="raw_ev")
        if is_z:
            p.dma("q", zg_scr[sl - 16], xsb[:], reads=[("xsb",)],
                  writes=[("zg",)], tag="zg_st")
            continue
        p.op("v", lambda e, sl=sl: e.tensor_scalar_mul(
            scr1[:, 0:T], raw[:, 4:4 + T], cwcol[:, sl * 4 + 3:sl * 4 + 4]),
            reads=[("raw",), ("cw",)], writes=[("scr1",)], tag="cv3")
        for j in range(3):
            p.op("v", lambda e, sl=sl, j=j: e.scalar_tensor_tensor(
                scr1[:, 0:T], raw[:, j + 1:j + 1 + T],
                cwcol[:, sl * 4 + j:sl * 4 + j + 1],
                scr1[:, 0:T], ALU.mult, ALU.add),
                reads=[("raw",), ("cw",), ("scr1",)], writes=[("scr1",)],
                tag="cvj")
        p.op("s", lambda e, sl=sl: e.activation(
            xsb[:], scr1[:, 0:T], AF.Silu, bias=cbcol[:, sl:sl + 1]),
            reads=[("scr1",), ("cb",)], writes=[("xsb",)], tag="xs_silu")
        p.dma("q", xs_scr[sl], xsb[:], reads=[("xsb",)], writes=[("xsscr",)],
              tag="xs_st")
        for tg in range(T // 512):
            bank, bk = cr.bank()
            p.op("t", lambda e, sl=sl, tg=tg, bank=bank: e.matmul(
                bank[0:96, :], xpw_sb[:, sl * 96:(sl + 1) * 96],
                xsb[:, tg * 512:(tg + 1) * 512], start=True, stop=True),
                reads=[("xpw",), ("xsb",)], writes=[bk], tag="xd_mm")
            p.op("v", lambda e, tg=tg, bank=bank: e.tensor_add(
                xdbl[0:96, tg * 512:(tg + 1) * 512],
                xdbl[0:96, tg * 512:(tg + 1) * 512], bank[0:96, :]),
                reads=[bk, ("xdbl",)], writes=[("xdbl",)], tag="xd_acc")

    xdbl_bf = p.sb("xdbl_bf", [128, T], BF16)
    p.op("v", lambda e: e.tensor_copy(xdbl_bf[0:64, :], xdbl[0:64, :]),
         reads=[("xdbl",)], writes=[("xdblbf",)], tag="xdbl_cast")
    for so in range(8):
        sl = sl0 + so
        for tg in range(T // 512):
            bank, bk = cr.bank()
            p.op("t", lambda e, sl=sl, tg=tg, bank=bank: e.matmul(
                bank[:], dtw_sb[0:64, sl * 128:(sl + 1) * 128],
                xdbl_bf[0:64, tg * 512:(tg + 1) * 512],
                start=True, stop=True),
                reads=[("dtw",), ("xdblbf",)], writes=[bk], tag="dt_mm")
            p.op("v", lambda e, tg=tg, bank=bank, so=so: e.tensor_scalar(
                scr1[:, tg * 512:(tg + 1) * 512], bank[:],
                dtbcol[:, (sl0 + so):(sl0 + so) + 1], None, ALU.add),
                reads=[bk, ("dtb",)], writes=[("scr1",)], tag="dt_b")
        p.op("s", lambda e: e.activation(scr2[:, 0:T], scr1[:, 0:T], AF.Abs),
             reads=[("scr1",)], writes=[("scr2",)], tag="sp_abs")
        p.op("s", lambda e: e.activation(scr2[:, 0:T], scr2[:, 0:T], AF.Exp,
                                         scale=-1.0),
             reads=[("scr2",)], writes=[("scr2",)], tag="sp_exp")
        p.op("s", lambda e: e.activation(scr2[:, 0:T], scr2[:, 0:T], AF.Ln,
                                         bias=1.0),
             reads=[("scr2",)], writes=[("scr2",)], tag="sp_ln")
        p.op("v", lambda e: e.tensor_scalar_max(scr3[:, 0:T], scr1[:, 0:T],
                                                0.0),
             reads=[("scr1",)], writes=[("scr3",)], tag="sp_relu")
        p.op("v", lambda e: e.tensor_add(scr1[:, 0:T], scr2[:, 0:T],
                                         scr3[:, 0:T]),
             reads=[("scr2",), ("scr3",)], writes=[("scr1",)], tag="sp_add")
        p.dma("q", dt_scr[so], scr1[:, 0:T], reads=[("scr1",)],
              writes=[("dtscr",)], tag="dt_st")
        p.dma("g", xsb[:], xs_scr[sl], reads=[("xsscr",)], writes=[("xsb",)],
              tag="xs_ld")
        p.op("v", lambda e: e.tensor_mul(scr2[:, 0:T], scr1[:, 0:T], xsb[:]),
             reads=[("scr1",), ("xsb",)], writes=[("scr2",)], tag="dtxs")
        p.dma("q", dtxs_scr[so], scr2[:, 0:T], reads=[("scr2",)],
              writes=[("dtxsscr",)], tag="dtxs_st")

    xdbl_scr = p.dtmp("xdbl_scr", (96, T))
    brow = p.sb("brow", [1, T], F32)
    p.dma("q", xdbl_scr[64:96, :], xdbl[64:96, :], reads=[("xdbl",)],
          writes=[("xdblscr",)], tag="xdbl_st")
    for part in range(2):
        for n in range(16):
            row = 64 + part * 16 + n
            p.dma("q", brow[:], xdbl_scr[row:row + 1, :], reads=[("xdblscr",)],
                  writes=[("brow",)], tag="brow_ld")
            for tg in range(T // 512):
                bank, bk = cr.bank()
                p.op("t", lambda e, tg=tg, bank=bank: e.matmul(
                    bank[:], cr.ones_row_f[0:1, 0:128],
                    brow[0:1, tg * 512:(tg + 1) * 512],
                    start=True, stop=True),
                    reads=[("brow",), ("ones_row_f",)], writes=[bk],
                    tag="bc_mm")
                p.op("v", lambda e, tg=tg, bank=bank: e.tensor_copy(
                    bcb[:, tg * 512:(tg + 1) * 512], bank[:]),
                    reads=[bk], writes=[("bcb",)], tag="bc_cp")
            p.dma("q", bc_scr[part, n], bcb[:, 0:T], reads=[("bcb",)],
                  writes=[("bcscr",)], tag="bc_st")

    aneg = p.sb("aneg", [128, 16], F32)
    for so in range(8):
        sl = sl0 + so
        p.dma("q", aneg[:], alog[sl * 128:(sl + 1) * 128, :],
              writes=[("aneg",)], tag="alog_ld")
        p.op("s", lambda e: e.activation(aneg[:], aneg[:], AF.Exp),
             reads=[("aneg",)], writes=[("aneg",)], tag="a_exp")
        p.op("v", lambda e: e.tensor_scalar_mul(aneg[:], aneg[:], -1.0),
             reads=[("aneg",)], writes=[("aneg",)], tag="a_neg")
        p.dma("q", scr1[:, 0:T], dt_scr[so], reads=[("dtscr",)],
              writes=[("scr1",)], tag="dt_ld")
        p.dma("q", scr2[:, 0:T], dtxs_scr[so], reads=[("dtxsscr",)],
              writes=[("scr2",)], tag="dtxs_ld")
        p.op("v", lambda e: e.memset(ybuf[:, 0:T], 0.0), writes=[("ybuf",)],
             tag="y0")
        for n in range(16):
            p.dma("g", bcb[:, 0:T], bc_scr[0, n], reads=[("bcscr",)],
                  writes=[("bcb",)], tag="bcb_ld")
            p.dma("g", bcb.ap()[:, T:2 * T], bc_scr[1, n], reads=[("bcscr",)],
                  writes=[("bcb",)], tag="bcc_ld")
            p.op("v", lambda e, n=n: e.tensor_scalar_mul(
                scr3[:, 0:T], scr1[:, 0:T], aneg[:, n:n + 1]),
                reads=[("scr1",), ("aneg",)], writes=[("scr3",)], tag="dA_m")
            p.op("s", lambda e: e.activation(scr3[:, 0:T], scr3[:, 0:T],
                                             AF.Exp),
                 reads=[("scr3",)], writes=[("scr3",)], tag="dA_e")
            p.op("v", lambda e: e.tensor_mul(raw[:, 4:4 + T], scr2[:, 0:T],
                                             bcb[:, 0:T]),
                 reads=[("scr2",), ("bcb",)], writes=[("raw",)], tag="u_m")
            p.op("v", lambda e: e.tensor_tensor_scan(
                raw[:, 4:4 + T], scr3[:, 0:T], raw[:, 4:4 + T], 0.0,
                ALU.mult, ALU.add),
                reads=[("scr3",), ("raw",)], writes=[("raw",)], tag="scan")
            p.op("v", lambda e: e.tensor_mul(raw[:, 4:4 + T], raw[:, 4:4 + T],
                                             bcb.ap()[:, T:2 * T]),
                 reads=[("raw",), ("bcb",)], writes=[("raw",)], tag="hC")
            p.op("v", lambda e: e.tensor_add(ybuf[:, 0:T], ybuf[:, 0:T],
                                             raw[:, 4:4 + T]),
                 reads=[("raw",), ("ybuf",)], writes=[("ybuf",)], tag="y_acc")
        p.dma("g", xsb[:], xs_scr[sl], reads=[("xsscr",)], writes=[("xsb",)],
              tag="xs_ld2")
        p.op("v", lambda e, so=so: e.scalar_tensor_tensor(
            ybuf[:, 0:T], xsb[:], dcol[:, sl0 + so:sl0 + so + 1],
            ybuf[:, 0:T], ALU.mult, ALU.add),
            reads=[("xsb",), ("dc",), ("ybuf",)], writes=[("ybuf",)],
            tag="y_xsD")
        p.dma("g", xsb[:], zg_scr[so], reads=[("zg",)], writes=[("xsb",)],
              tag="zg_ld")
        p.op("v", lambda e: e.tensor_mul(ybuf[:, 0:T], ybuf[:, 0:T], xsb[:]),
             reads=[("xsb",), ("ybuf",)], writes=[("ybuf",)], tag="y_z")
        p.op("v", lambda e: e.tensor_copy(xsb[:], ybuf[:, 0:T]),
             reads=[("ybuf",)], writes=[("xsb",)], tag="y_cast")
        p.dma("q", y_out[so], xsb[:], reads=[("xsb",)], tag="y_st")
    st_op = p.all_ops[-1]
    st_op.needs_inc = True
    nop = p.op("q", lambda e: e.nop(), tag="final_nop")
    nop.deps.append(st_op)
    return p.build(), None


def build_l2_attn_a():
    p, cr, aps = begin_launch()
    xn_out = p.dout("xn_out", (OWN, 1024), BF16)
    xnt = cr.scratch("xnt", [128, D], BF16)
    for t in range(NT):
        x_ap = cr.x_sb[:, t * D:(t + 1) * D]
        emit_ln_tile(cr, x_ap, [("x", t)], [(xnt[:], ("xnt",))], sidx=t % 2)
        p.dma("q", xn_out[t * 128:(t + 1) * 128, :], xnt[:],
              reads=[("xnt",)], tag="xn_st")
    return finish_launch(p, cr, aps), aps


def build_l2_attn_c():
    p, cr, aps = begin_launch()
    y_fm = p.din("y_fm", (16, 128, OWN), BF16)
    opT = p.din("l2a_opT", (2048, 1024))
    yfm_sb = p.sb("yfm_sb", [128, 16 * 128], BF16)
    op_sb = p.sb("op_sb", [128, 16 * 1024], BF16)
    yt = cr.scratch("yt", [128, 1024], F32)
    p.dma("g", op_sb.ap().rearrange("p (s n) -> p s n", s=16),
          opT.rearrange("(s p) n -> p s n", p=128), writes=[("op",)])
    for t in range(NT):
        p.dma("q", yfm_sb.ap().rearrange("p (s n) -> p s n", s=16),
              y_fm[:, :, t * 128:(t + 1) * 128].rearrange("s p n -> p s n"),
              writes=[("yfm",)], tag="yfm_ld")
        for dg in range(2):
            bank, bk = cr.bank()
            for s_ in range(16):
                p.op("t", lambda e, s_=s_, dg=dg, bank=bank: e.matmul(
                    bank[:],
                    yfm_sb.ap().rearrange("p (s n) -> p s n", s=16)[:, s_],
                    op_sb[:, s_ * 1024 + dg * 512: s_ * 1024 + dg * 512 + 512],
                    start=(s_ == 0), stop=(s_ == 15)),
                    reads=[("yfm",), ("op",)], writes=[bk], tag="op_mm")
            p.op("v", lambda e, dg=dg, bank=bank: e.tensor_copy(
                yt[:, dg * 512:(dg + 1) * 512], bank[:]),
                reads=[bk], writes=[("yt",)], tag="op_ev")
        emit_residual_ln_add(cr, yt.ap(), [("yt",)], t, sidx=2)
    return finish_launch(p, cr, aps), aps


def run_l2_attn(x_shards, inp, run):
    consts = np_consts()
    extra = l2_attn_host_inputs(inp)
    nc_a, _ = build_l2_attn_a()
    in_maps = [dict(x_in=x_shards[c], consts=consts) for c in range(NCORES)]
    resA = run(nc_a, in_maps)
    xn_full = {}
    for b_ in range(B):
        xf = np.zeros((T, 1024), dtype=BF)
        for par in range(2):
            xf[own_index(par)] = resA[2 * b_ + par]["xn_out"]
        xn_full[b_] = host_fm_rearrange(xf)
    nc_b = {}
    for par in range(2):
        nc_b[par], _ = build_l2_attn_b(par)
    y_half = [None] * NCORES
    for par in range(2):
        cores = [2 * b_ + par for b_ in range(B)]
        in_maps = []
        for b_ in range(B):
            m = dict(consts=consts, xn_fm=xn_full[b_])
            for k in ("l2a_inpT", "l2a_xpwT", "l2a_dtwT", "l2a_convwT",
                      "l2a_convbT", "l2a_dtbT", "l2a_alog", "l2a_DT"):
                m[k] = extra[k]
            in_maps.append(m)
        resB = run(nc_b[par], in_maps, core_ids=cores)
        for i, c in enumerate(cores):
            y_half[c] = resB[i]["y_out"]
    nc_c, _ = build_l2_attn_c()
    in_maps = []
    for c in range(NCORES):
        b_, par = c // 2, c % 2
        idx = own_index(par)
        yf = np.concatenate([y_half[2 * b_], y_half[2 * b_ + 1]], axis=0)
        yf_own = np.ascontiguousarray(yf[:, :, idx])
        in_maps.append(dict(x_in=x_shards[c], consts=consts, y_fm=yf_own,
                            l2a_opT=extra["l2a_opT"]))
    resC = run(nc_c, in_maps)
    return [resC[c]["x_out"] for c in range(NCORES)]



def l0_ffn_host_inputs(inp):
    f32 = np.float32
    d = {
        "l0f_Wqkv": np.ascontiguousarray(inp["L0_ffn_Wqkv"]).astype(f32),
        "l0f_Wo": np.ascontiguousarray(inp["L0_ffn_Wo"]).astype(f32),
        "l0f_qs": inp["L0_ffn_q_scale"].astype(f32),
        "l0f_ks": inp["L0_ffn_k_scale"].astype(f32),
        "l0f_lng": inp["L0_ffn_ln_g"].astype(f32),
        "l0f_lnb": inp["L0_ffn_ln_b"].astype(f32),
        "l0f_b2": inp["L0_ffn_b2"].astype(f32),
    }
    W1 = inp["L0_ffn_W1"].astype(f32)
    b1 = inp["L0_ffn_b1"].astype(f32)
    W2 = inp["L0_ffn_W2"].astype(f32)
    W1p = np.zeros((6, 1024, 5632), f32)
    b1p = np.zeros((6, 5632), f32)
    W2p = np.zeros((6, 2816, 1024), f32)
    W1p[:, :, 0:2730] = W1[:, :, 0:2730]
    W1p[:, :, 2816:2816 + 2730] = W1[:, :, 2730:5460]
    b1p[:, 0:2730] = b1[:, 0:2730]
    b1p[:, 2816:2816 + 2730] = b1[:, 2730:5460]
    W2p[:, 0:2730, :] = W2
    d["l0f_W1p"] = W1p
    d["l0f_b1p"] = b1p
    d["l0f_W2p"] = W2p
    half = np.arange(0, 64, 2, dtype=np.float64)
    invf = 1.0 / 10000.0 ** (half / 64)
    tt = np.arange(T, dtype=np.float64)
    fr = tt[:, None] * invf[None, :]
    base = (half + 0.4 * 64) / (1.4 * 64)
    power = (tt - T // 2) / 128.0
    scale = base[None, :] ** power[:, None]
    cq = (np.cos(fr) * scale * 8.0)
    sq = (np.sin(fr) * scale * 8.0)
    ck = (np.cos(fr) / scale)
    sk = (np.sin(fr) / scale)
    tile16 = lambda a: np.ascontiguousarray(np.tile(a, (1, 16)).astype(f32))
    for c in range(NCORES):
        par = c % 2
        pos = own_index(par)
        hstart = 2816 if par == 0 else 768
        hpos = np.arange(hstart, hstart + 256)
        d.setdefault("_tabs", {})[c] = {
            "cq": tile16(cq[pos]), "sq": tile16(sq[pos]),
            "ck": tile16(ck[pos]), "sk": tile16(sk[pos]),
            "ckh": tile16(ck[hpos]), "skh": tile16(sk[hpos]),
        }
    d["l0f_tabs"] = (lambda m: (lambda c: m[c]))(d.pop("_tabs"))
    return d


def halo_tiles_abs(parity):
    return [2816, 2944] if parity == 0 else [768, 896]


def build_l0_ffn_attn(parity):
    p = Prog()
    consts = p.din("consts", (641, 1536))
    xs_in = p.din("xs_in", (OWN, D))
    xsh_in = p.din("xsh_in", (256, D))
    xp_in = p.din("xp_in", (OWN, D))
    wqkv = p.din("l0f_Wqkv_i", (1024, 3072))
    wo = p.din("l0f_Wo_i", (1024, 1024))
    qs = p.din("l0f_qs_i", (1, 64))
    ks = p.din("l0f_ks_i", (1, 64))
    tabs = {nm: p.din(f"l0f_{nm}", (OWN, 512)) for nm in ("cq", "sq", "ck", "sk")}
    tabsh = {nm: p.din(f"l0f_{nm}h", (256, 512)) for nm in ("ck", "sk")}
    xp_out = p.dout("x_out", (OWN, D))
    cr = Core(p)
    register_float_consts(cr, FLOAT_CONSTS)
    cr.load_consts(consts)
    load_masks(cr, consts, band=True)

    q_scr = p.dtmp("q_scr", (8, 128, OWN), BF16)
    k_scr = p.dtmp("k_scr", (8, 128, T), BF16)
    v_scr = p.dtmp("v_scr", (T, 1024), BF16)

    wq_sb = p.sb("wq_sb", [128, 2 * 8 * 512], BF16)
    wo_sb = p.sb("wo_sb", [128, 8 * 1024], BF16)
    xst = cr.scratch("xst", [128, D], F32)
    xsb = cr.scratch("xsb", [128, D], BF16)
    xsfm = p.sb("xsfm", [128, 8 * 128], BF16)
    qkvf = p.sb("qkvf", [128, 3072], F32)
    rst = p.sb("rst", [128, 48], F32)
    qsb = p.sb("qsb", [128, 2 * 1024], F32)
    qrow = p.sb("qrow", [1, 2 * 1024], F32)
    ctb = p.sb("ctb", [128, 2 * 512], F32)
    qtb = cr.scratch("qtb", [128, 1024], BF16)
    qfm = p.sb("qfm", [128, 8 * 128], BF16)
    attn_fm = p.sb("attn_fm", [128, 8 * OWN], BF16)
    sqs = cr.scratch("sqs", [128, 1024], F32)
    yt = cr.scratch("yt", [128, 1024], F32)
    xpt = cr.scratch("xpt", [128, 1024], F32)

    p.dma("g", wo_sb.ap().rearrange("p (s n) -> p s n", s=8),
          wo.rearrange("(s p) n -> p s n", p=128), writes=[("wo",)])
    for j, (src, part) in enumerate(((qs, 0), (ks, 1))):
        for rep in range(16):
            p.dma("q", qrow[0:1, part * 1024 + rep * 64: part * 1024 + rep * 64 + 64],
                  src, writes=[("qrow",)], tag="qsrow")
    for part in range(2):
        for dg in range(2):
            bank, bk = cr.bank()
            p.op("t", lambda e, part=part, dg=dg, bank=bank: e.matmul(
                bank[:], cr.ones_row_f[0:1, 0:128],
                qrow[0:1, part * 1024 + dg * 512: part * 1024 + dg * 512 + 512],
                start=True, stop=True),
                reads=[("qrow",), ("ones_row_f",)], writes=[bk], tag="qs_bc")
            p.op("v", lambda e, part=part, dg=dg, bank=bank: e.tensor_copy(
                qsb[:, part * 1024 + dg * 512: part * 1024 + dg * 512 + 512],
                bank[:]),
                reads=[bk], writes=[("qsb",)], tag="qs_cp")

    ht_abs = halo_tiles_abs(parity)
    for ti in range(NT + 2):
        is_halo = ti >= NT
        if is_halo:
            hi = ti - NT
            src = xsh_in[hi * 128:(hi + 1) * 128, :]
            abs_pos = ht_abs[hi]
        else:
            src = xs_in[ti * 128:(ti + 1) * 128, :]
            abs_pos = tile_pos(parity, ti)
        p.dma("q", xst[:], src, writes=[("xst",)], tag="xs_ld")
        p.op("v", lambda e: e.tensor_copy(xsb[:], xst[:]),
             reads=[("xst",)], writes=[("xsb",)], tag="xs_cast")
        emit_tok_to_fm(cr, xsb[:], [("xsb",)], 0, xsfm, ("xsfm",), ntok=128)
        for og in range(6):
            bi = og % 2
            p.dma("g", wq_sb.ap().rearrange("p (b s n) -> p b s n", b=2, s=8)
                  [:, bi],
                  wqkv.rearrange("(s p) n -> p s n", p=128)
                  [:, :, og * 512:(og + 1) * 512],
                  writes=[("wq", bi)], tag="wq_ld")
            bank, bk = cr.bank()
            for s_ in range(8):
                p.op("t", lambda e, s_=s_, og=og, bank=bank, bi=bi: e.matmul(
                    bank[:], xsfm[:, s_ * 128:(s_ + 1) * 128],
                    wq_sb.ap().rearrange("p (b s n) -> p b s n", b=2, s=8)
                    [:, bi, s_],
                    start=(s_ == 0), stop=(s_ == 7)),
                    reads=[("xsfm",), ("wq", bi)], writes=[bk], tag="qkv_mm")
            p.op("v", lambda e, og=og, bank=bank: e.tensor_copy(
                qkvf[:, og * 512:(og + 1) * 512], bank[:]),
                reads=[bk], writes=[("qkvf",)], tag="qkv_ev")
        for which, base_c in (("q", 0), ("k", 1024)):
            if which == "q" and is_halo:
                continue
            qv = qkvf[:, base_c:base_c + 1024]
            p.op("s", lambda e, qv=qv: e.activation(sqs[:], qv, AF.Square),
                 reads=[("qkvf",)], writes=[("sqs",)], tag="l2_sq")
            p.op("v", lambda e: e.reduce_sum(
                rst[:, 0:16], sqs.ap().rearrange("p (h j) -> p h j", j=64),
                axis=AX.X),
                reads=[("sqs",)], writes=[("rst",)], tag="l2_red")
            p.op("s", lambda e: e.activation(rst[:, 16:32], rst[:, 0:16],
                                             AF.Sqrt),
                 reads=[("rst",)], writes=[("rst",)], tag="l2_rt")
            p.op("v", lambda e: e.tensor_scalar_max(rst[:, 16:32],
                                                    rst[:, 16:32], 1e-12),
                 reads=[("rst",)], writes=[("rst",)], tag="l2_mx")
            p.op("v", lambda e: e.reciprocal(rst[:, 32:48], rst[:, 16:32]),
                 reads=[("rst",)], writes=[("rst",)], tag="l2_rc")
            for h in range(16):
                p.op("v", lambda e, h=h, qv=qv: e.tensor_scalar_mul(
                    qv[:, h * 64:(h + 1) * 64], qv[:, h * 64:(h + 1) * 64],
                    rst[:, 32 + h:33 + h]),
                    reads=[("qkvf",), ("rst",)], writes=[("qkvf",)],
                    tag="l2_ap")
            part = 0 if which == "q" else 1
            p.op("v", lambda e, qv=qv, part=part: e.tensor_mul(
                qv, qv, qsb[:, part * 1024:(part + 1) * 1024]),
                reads=[("qkvf",), ("qsb",)], writes=[("qkvf",)], tag="l2_qs")
            if which == "q":
                tc, tsn = tabs["cq"], tabs["sq"]
                off = ti * 128
            else:
                if is_halo:
                    tc, tsn = tabsh["ck"], tabsh["sk"]
                    off = (ti - NT) * 128
                else:
                    tc, tsn = tabs["ck"], tabs["sk"]
                    off = ti * 128
            p.dma("q", ctb[:, 0:512], tc[off:off + 128, :], writes=[("ctb",)],
                  tag="ct_ld")
            p.dma("q", ctb[:, 512:1024], tsn[off:off + 128, :],
                  writes=[("ctb",)], tag="st_ld")
            emit_rope_halfsplit(cr, qv, ("qkvf",), qtb.ap(), ("qtb",),
                                ctb.ap()[:, 0:512], ctb.ap()[:, 512:1024],
                                ("ctb",), -1, +1)
            emit_tok_to_fm(cr, qtb[:], [("qtb",)], 0, qfm, ("qfm",), ntok=128)
            if which == "q":
                p.dma("q", q_scr[:, :, ti * 128:(ti + 1) * 128].rearrange(
                    "s p n -> p s n"),
                    qfm.ap().rearrange("p (s n) -> p s n", s=8),
                    reads=[("qfm",)], writes=[("qscr",)], tag="q_st")
            else:
                p.dma("q", k_scr[:, :, abs_pos:abs_pos + 128].rearrange(
                    "s p n -> p s n"),
                    qfm.ap().rearrange("p (s n) -> p s n", s=8),
                    reads=[("qfm",)], writes=[("kscr",)], tag="k_st")
        p.op("v", lambda e: e.tensor_copy(qtb[:], qkvf[:, 2048:3072]),
             reads=[("qkvf",)], writes=[("qtb",)], tag="v_cast")
        p.dma("q", v_scr[abs_pos:abs_pos + 128, :], qtb[:],
              reads=[("qtb",)], writes=[("vscr",)], tag="v_st")

    kranges = ([(0, 1024), (2816, 4096)] if parity == 0 else [(768, 3072)])
    attn_cfg = dict(
        nh=16, dq=64, dv=64, causal=True, parity=parity, band=256,
        k_ranges=kranges,
        q_parts=[(lambda h: q_scr[h // 2, (h % 2) * 64:(h % 2) * 64 + 64, :],
                  0, 64)],
        k_parts=[(lambda h: k_scr[h // 2, (h % 2) * 64:(h % 2) * 64 + 64, :],
                  0, 64)],
        k_extra_reads=[("kscr",)], v_tok=v_scr, v_extra_reads=[("vscr",)],
        v_arr=None, attn_fm=attn_fm)
    emit_attention(cr, attn_cfg)

    for t in range(NT):
        for dg in range(2):
            bank, bk = cr.bank()
            for s_ in range(8):
                p.op("t", lambda e, s_=s_, dg=dg, bank=bank, t=t: e.matmul(
                    bank[:],
                    attn_fm.ap().rearrange("p (s n) -> p s n", n=OWN)[
                        :, s_, t * 128:(t + 1) * 128],
                    wo_sb[:, s_ * 1024 + dg * 512: s_ * 1024 + dg * 512 + 512],
                    start=(s_ == 0), stop=(s_ == 7)),
                    reads=[("attnfm",), ("wo",)], writes=[bk], tag="wo_mm")
            p.op("v", lambda e, dg=dg, bank=bank: e.tensor_copy(
                yt[:, dg * 512:(dg + 1) * 512], bank[:]),
                reads=[bk], writes=[("yt",)], tag="wo_ev")
        p.dma("q", xpt[:], xp_in[t * 128:(t + 1) * 128, :],
              writes=[("xpt",)], tag="xp_ld")
        p.op("v", lambda e: e.tensor_add(yt[:], yt[:], xpt[:]),
             reads=[("yt",), ("xpt",)], writes=[("yt",)], tag="resid")
        p.dma("q", xp_out[t * 128:(t + 1) * 128, :], yt[:],
              reads=[("yt",)], tag="x_st")
    st_op = p.all_ops[-1]
    st_op.needs_inc = True
    nop = p.op("q", lambda e: e.nop(), tag="final_nop")
    nop.deps.append(st_op)
    return p.build(), None


def build_l0_ffn_ffn(last):
    p = Prog()
    consts = p.din("consts", (641, 1536))
    xs_in = p.din("xs_in", (OWN, D))
    xp_in = p.din("xp_in", (OWN, D))
    lng = p.din("l0f_lng_i", (1, 1024))
    lnb = p.din("l0f_lnb_i", (1, 1024))
    w1 = p.din("l0f_W1p_i", (1024, 5632))
    b1 = p.din("l0f_b1p_i", (1, 5632))
    w2 = p.din("l0f_W2p_i", (2816, 1024))
    b2 = p.din("l0f_b2_i", (1, 1024))
    if last:
        xorig_in = p.din("xorig_in", (OWN, D))
    x_out = p.dout("x_out", (OWN, D))
    cr = Core(p)
    register_float_consts(cr, FLOAT_CONSTS)
    cr.load_consts(consts)

    w1_sb = p.sb("w1_sb", [128, 2 * 8 * 512], BF16)
    w2_sb = p.sb("w2_sb", [128, 22 * 1024], BF16)
    b1_sb = p.sb("b1_sb", [1, 5632], BF16)
    b2_sb = p.sb("b2_sb", [1, 1024], BF16)
    gb = p.sb("gb", [128, 2 * 1024], F32)
    grow = p.sb("grow", [1, 2048], F32)
    xst = cr.scratch("xst", [128, D], F32)
    xnb = cr.scratch("xnb", [128, D], BF16)
    xsfm = p.sb("xsfm", [128, 8 * 128], BF16)
    hgf = p.sb("hgf", [128, 5632], F32)
    hb = p.sb("hb", [128, 2816], BF16)
    gg = p.sb("gg", [128, 2816], BF16)
    hfm = p.sb("hfm", [128, 22 * 128], BF16)
    yt = cr.scratch("yt", [128, 1024], F32)
    xpt = cr.scratch("xpt", [128, 1024], F32)
    xo = cr.scratch("xo", [128, 1024], F32)

    p.dma("g", w2_sb.ap().rearrange("p (s n) -> p s n", s=22),
          w2.rearrange("(s p) n -> p s n", p=128), writes=[("w2",)])
    p.dma("g", b1_sb[:], b1, writes=[("b1",)])
    p.dma("g", b2_sb[:], b2, writes=[("b2",)])
    p.dma("q", grow[0:1, 0:1024], lng, writes=[("grow",)])
    p.dma("q", grow[0:1, 1024:2048], lnb, writes=[("grow",)])
    for part in range(2):
        for dg in range(2):
            bank, bk = cr.bank()
            p.op("t", lambda e, part=part, dg=dg, bank=bank: e.matmul(
                bank[:], cr.ones_row_f[0:1, 0:128],
                grow[0:1, part * 1024 + dg * 512: part * 1024 + dg * 512 + 512],
                start=True, stop=True),
                reads=[("grow",), ("ones_row_f",)], writes=[bk], tag="g_bc")
            p.op("v", lambda e, part=part, dg=dg, bank=bank: e.tensor_copy(
                gb[:, part * 1024 + dg * 512: part * 1024 + dg * 512 + 512],
                bank[:]),
                reads=[bk], writes=[("gb",)], tag="g_cp")

    for t in range(NT):
        p.dma("q", xst[:], xs_in[t * 128:(t + 1) * 128, :],
              writes=[("xst",)], tag="xs_ld")
        emit_ln_tile(cr, xst.ap(), [("xst",)], [(yt[:], ("ytl",))],
                     sidx=t % 2)
        p.op("v", lambda e: e.tensor_mul(yt[:], yt[:], gb[:, 0:1024]),
             reads=[("ytl",), ("gb",)], writes=[("ytl",)], tag="ln_g")
        p.op("v", lambda e: e.tensor_add(yt[:], yt[:], gb[:, 1024:2048]),
             reads=[("ytl",), ("gb",)], writes=[("ytl",)], tag="ln_b")
        p.op("v", lambda e: e.tensor_copy(xnb[:], yt[:]),
             reads=[("ytl",)], writes=[("xnb",)], tag="xn_cast")
        emit_tok_to_fm(cr, xnb[:], [("xnb",)], 0, xsfm, ("xsfm",), ntok=128)
        for og in range(11):
            bi = og % 2
            p.dma("g", w1_sb.ap().rearrange("p (b s n) -> p b s n", b=2, s=8)
                  [:, bi],
                  w1.rearrange("(s p) n -> p s n", p=128)
                  [:, :, og * 512:(og + 1) * 512],
                  writes=[("w1", bi)], tag="w1_ld")
            bank, bk = cr.bank()
            for s_ in range(8):
                p.op("t", lambda e, s_=s_, og=og, bank=bank, bi=bi: e.matmul(
                    bank[:], xsfm[:, s_ * 128:(s_ + 1) * 128],
                    w1_sb.ap().rearrange("p (b s n) -> p b s n", b=2, s=8)
                    [:, bi, s_],
                    start=(s_ == 0), stop=False),
                    reads=[("xsfm",), ("w1", bi)], writes=[bk], tag="h_mm")
            p.op("t", lambda e, og=og, bank=bank: e.matmul(
                bank[:], cr.ones_row[0:1, 0:128],
                b1_sb[0:1, og * 512:(og + 1) * 512], start=False, stop=True),
                reads=[("b1",), ("ones_row",)], writes=[bk], tag="h_b1")
            p.op("v", lambda e, og=og, bank=bank: e.tensor_copy(
                hgf[:, og * 512:(og + 1) * 512], bank[:]),
                reads=[bk], writes=[("hgf",)], tag="h_ev")
        p.op("s", lambda e: e.activation(gg[:], hgf[:, 2816:5632], AF.Gelu),
             reads=[("hgf",)], writes=[("gg",)], tag="gelu")
        p.op("v", lambda e: e.tensor_copy(hb[:], hgf[:, 0:2816]),
             reads=[("hgf",)], writes=[("hb",)], tag="a_cast")
        p.op("v", lambda e: e.tensor_mul(hb[:], hb[:], gg[:]),
             reads=[("hb",), ("gg",)], writes=[("hb",)], tag="h_mul")
        for s0 in range(0, 22, 8):
            ns = min(8, 22 - s0)
            bankT, bkT = cr.tbank()
            for s_ in range(ns):
                p.op("t", lambda e, s_=s_, s0=s0, bankT=bankT: e.matmul(
                    bankT[:, s_ * 128:(s_ + 1) * 128],
                    hb[:, (s0 + s_) * 128:(s0 + s_ + 1) * 128],
                    cr.identb[:], start=True, stop=True, is_transpose=True),
                    reads=[("hb",), ("identb",)], writes=[bkT], tag="h_tr")
            p.op("v", lambda e, s0=s0, ns=ns, bankT=bankT: e.tensor_copy(
                hfm[:, s0 * 128:(s0 + ns) * 128], bankT[:, 0:ns * 128]),
                reads=[bkT], writes=[("hfmk",)], tag="h_trcp")
        for dg in range(2):
            bank, bk = cr.bank()
            for s_ in range(22):
                p.op("t", lambda e, s_=s_, dg=dg, bank=bank: e.matmul(
                    bank[:], hfm[:, s_ * 128:(s_ + 1) * 128],
                    w2_sb[:, s_ * 1024 + dg * 512: s_ * 1024 + dg * 512 + 512],
                    start=(s_ == 0), stop=False),
                    reads=[("hfmk",), ("w2",)], writes=[bk], tag="d_mm")
            p.op("t", lambda e, dg=dg, bank=bank: e.matmul(
                bank[:], cr.ones_row[0:1, 0:128],
                b2_sb[0:1, dg * 512:(dg + 1) * 512], start=False, stop=True),
                reads=[("b2",), ("ones_row",)], writes=[bk], tag="d_b2")
            p.op("v", lambda e, dg=dg, bank=bank: e.tensor_copy(
                yt[:, dg * 512:(dg + 1) * 512], bank[:]),
                reads=[bk], writes=[("ytl",)], tag="d_ev")
        p.dma("q", xpt[:], xp_in[t * 128:(t + 1) * 128, :],
              writes=[("xpt",)], tag="xp_ld")
        p.op("v", lambda e: e.tensor_add(yt[:], yt[:], xpt[:]),
             reads=[("ytl",), ("xpt",)], writes=[("ytl",)], tag="resid")
        if last:
            emit_ln_tile(cr, yt.ap(), [("ytl",)], [(xo[:], ("xok",))],
                         sidx=2)
            p.dma("q", xpt[:], xorig_in[t * 128:(t + 1) * 128, :],
                  writes=[("xpt",)], tag="xo_ld")
            p.op("v", lambda e: e.tensor_add(xo[:], xo[:], xpt[:]),
                 reads=[("xok",), ("xpt",)], writes=[("xok",)], tag="fold")
            p.dma("q", x_out[t * 128:(t + 1) * 128, :], xo[:],
                  reads=[("xok",)], tag="x_st")
        else:
            p.dma("q", x_out[t * 128:(t + 1) * 128, :], yt[:],
                  reads=[("ytl",)], tag="x_st")
    st_op = p.all_ops[-1]
    st_op.needs_inc = True
    nop = p.op("q", lambda e: e.nop(), tag="final_nop")
    nop.deps.append(st_op)
    return p.build(), None


def host_shift(x_state_full, rows):
    a = x_state_full[rows, 0:512]
    prev = rows - 1
    s = np.where(prev[:, None] >= 0, x_state_full[np.maximum(prev, 0), 512:],
                 np.zeros((1, 512), x_state_full.dtype))
    return np.ascontiguousarray(np.concatenate([a, s], axis=1))


def run_l0_ffn(x_shards, inp, run):
    consts = np_consts()
    extra = l0_ffn_host_inputs(inp)
    nc_ln, _ = build_l0_ffn_ln()
    in_maps = [dict(x_in=x_shards[c], consts=consts) for c in range(NCORES)]
    res = run(nc_ln, in_maps)
    state = {b_: np.zeros((T, D), np.float32) for b_ in range(B)}
    for c in range(NCORES):
        state[c // 2][own_index(c % 2)] = res[c]["x_out"]

    nc_attn = {par: build_l0_ffn_attn(par)[0] for par in range(2)}
    nc_ffn = {False: build_l0_ffn_ffn(False)[0], True: build_l0_ffn_ffn(True)[0]}
    for i in range(6):
        new_state = {b_: state[b_].copy() for b_ in range(B)}
        for par in range(2):
            cores = [2 * b_ + par for b_ in range(B)]
            hstart = 2816 if par == 0 else 768
            in_maps = []
            for b_ in range(B):
                tabs = extra["l0f_tabs"](2 * b_ + par)
                idx = own_index(par)
                in_maps.append(dict(
                    consts=consts,
                    xs_in=host_shift(state[b_], idx),
                    xsh_in=host_shift(state[b_], np.arange(hstart, hstart + 256)),
                    xp_in=np.ascontiguousarray(state[b_][idx]),
                    l0f_Wqkv_i=extra["l0f_Wqkv"][i],
                    l0f_Wo_i=extra["l0f_Wo"][i],
                    l0f_qs_i=extra["l0f_qs"][i:i + 1],
                    l0f_ks_i=extra["l0f_ks"][i:i + 1],
                    l0f_cq=tabs["cq"], l0f_sq=tabs["sq"],
                    l0f_ck=tabs["ck"], l0f_sk=tabs["sk"],
                    l0f_ckh=tabs["ckh"], l0f_skh=tabs["skh"]))
            resA = run(nc_attn[par], in_maps, core_ids=cores)
            for j, c in enumerate(cores):
                new_state[c // 2][own_index(par)] = resA[j]["x_out"]
        state = new_state
        last = (i == 5)
        new_state = {b_: state[b_].copy() for b_ in range(B)}
        for par in range(2):
            cores = [2 * b_ + par for b_ in range(B)]
            in_maps = []
            for b_ in range(B):
                idx = own_index(par)
                m = dict(consts=consts,
                         xs_in=host_shift(state[b_], idx),
                         xp_in=np.ascontiguousarray(state[b_][idx]),
                         l0f_lng_i=extra["l0f_lng"][i:i + 1],
                         l0f_lnb_i=extra["l0f_lnb"][i:i + 1],
                         l0f_W1p_i=extra["l0f_W1p"][i],
                         l0f_b1p_i=extra["l0f_b1p"][i:i + 1].reshape(1, 5632),
                         l0f_W2p_i=extra["l0f_W2p"][i],
                         l0f_b2_i=extra["l0f_b2"][i:i + 1])
                if last:
                    m["xorig_in"] = x_shards[2 * b_ + par]
                in_maps.append(m)
            resF = run(nc_ffn[last], in_maps, core_ids=cores)
            for j, c in enumerate(cores):
                new_state[c // 2][own_index(par)] = resF[j]["x_out"]
        state = new_state
    return [np.ascontiguousarray(state[c // 2][own_index(c % 2)])
            for c in range(NCORES)]


def build_l0_ffn_ln():
    p, cr, aps = begin_launch()
    xnt = cr.scratch("xnf32", [128, D], F32)
    for t in range(NT):
        x_ap = cr.x_sb[:, t * D:(t + 1) * D]
        emit_ln_tile(cr, x_ap, [("x", t)], [(xnt[:], ("xnf32",))], sidx=t % 2)
        p.dma("q", aps["x_out"][t * 128:(t + 1) * 128, :], xnt[:],
              reads=[("xnf32",)], tag="xn_st")
    st_op = p.all_ops[-1]
    st_op.needs_inc = True
    nop = p.op("q", lambda e: e.nop(), tag="final_nop")
    nop.deps.append(st_op)
    return p.build(), None



def shard_x(x_full):
    return [np.ascontiguousarray(x_full[c // 2][own_index(c % 2)])
            for c in range(NCORES)]


def unshard_x(shards):
    out = np.zeros((B, T, D), dtype=np.float32)
    for c in range(NCORES):
        out[c // 2][own_index(c % 2)] = shards[c]
    return out


_RUN = None


def _device_run(nc, in_maps, core_ids=None):
    if core_ids is None:
        core_ids = list(range(len(in_maps)))
    res = run_bass_kernel_spmd(nc, in_maps, core_ids=core_ids)
    return res.results


def kernel(**inputs):
    inputs = {k: np.asarray(v) for k, v in inputs.items()}
    x = np.ascontiguousarray(inputs["L0_attn_x"].astype(np.float32))
    shards = shard_x(x)
    run = _RUN or _device_run
    shards = run_l0_attn(shards, inputs, run)
    shards = [np.asarray(s) for s in shards]
    shards = run_l0_ffn(shards, inputs, run)
    shards = run_l1_attn(shards, inputs, run)
    shards = run_l1_ffn(shards, inputs, run)
    shards = run_l2_attn(shards, inputs, run)
    for name in ("l2_ffn",):
        nc, _ = build_l2_ffn()
        extra = l2_ffn_host_inputs(inputs)
        in_maps = [dict(x_in=np.ascontiguousarray(shards[c]),
                        consts=np_consts(), **extra) for c in range(NCORES)]
        res = run(nc, in_maps)
        shards = [res[c]["x_out"] for c in range(NCORES)]
    shards = run_l3_attn(shards, inputs, run)
    nc, _ = build_l3_ffn()
    extra = l3_ffn_host_inputs(inputs)
    in_maps = [dict(x_in=np.ascontiguousarray(shards[c]), consts=np_consts(),
                    **extra) for c in range(NCORES)]
    res = run(nc, in_maps)
    shards = [res[c]["x_out"] for c in range(NCORES)]
    return unshard_x([np.asarray(s) for s in shards])
